# version 09
# baseline (speedup 1.0000x reference)
import math
import contextlib
import numpy as np
import ml_dtypes
import concourse.bass as bass
import concourse.mybir as mybir
from concourse.bass_utils import run_bass_kernel_spmd

F32 = mybir.dt.float32
BF16 = mybir.dt.bfloat16
I32 = mybir.dt.int32
U32 = mybir.dt.uint32
AF = mybir.ActivationFunctionType
ALU = mybir.AluOpType
AX = mybir.AxisListType

B, S, D = 4, 4096, 2048
NT, NOWN = 32, 16
H, KV, DH, RD = 16, 4, 128, 64
E, MI = 8, 1024
THETA = 1e6
EPS_LN = 1e-5
EPS_RMS = 1e-6
NEG = -1e30
SCALE_B = float(1.0 / math.sqrt(8.0))
SCALE_F = float(1.0 / math.sqrt(128.0))

ENGS = ("sync", "vector", "scalar", "tensor", "gpsimd")


def own_tiles(j):
    out = []
    for u in range(8):
        out += ([4 * u, 4 * u + 3] if j == 0 else [4 * u + 1, 4 * u + 2])
    return out


CAPS = []
for _u in range(8):
    CAPS += [4 * _u + 2, 4 * _u + 4]

TWO_PI_HI = float(np.float32(2 * np.pi))
_r1 = 2 * np.pi - np.float64(TWO_PI_HI)
TWO_PI_MID = float(np.float32(_r1))
TWO_PI_LO = float(_r1 - np.float64(TWO_PI_MID))


import os
DEDUP = os.environ.get("KDEDUP", "1") == "1"
KTRUNC = int(os.environ.get("KTRUNC", "0"))
KONEQ = os.environ.get("KONEQ", "0") == "1"
KSKIP = int(os.environ.get("KSKIP", "0"))
HAZN = int(os.environ.get("KHAZN", "4"))


class Plan:
    def __init__(self):
        self.ops = []
        self.cnt = {}
        for e in ENGS:
            self.cnt["C" + e] = 0
            self.cnt["D" + e] = 0
        self.hiwater = {e: {} for e in ENGS}

    def fence(self, toks):
        self.pending = list(toks)

    def op(self, eng, fn, deps=(), dma=False):
        if dma and KONEQ and eng == "sync":
            eng = "gpsimd"
        sem = ("D" if dma else "C") + eng
        inc = 16 if dma else 1
        waits = []
        hw = self.hiwater[eng]
        for (s_, v) in list(deps) + getattr(self, "pending", []):
            if s_ == "C" + eng and not dma:
                if v <= self.cnt[s_] - HAZN:
                    continue
            if dma and s_ == "D" + eng:
                continue
            if DEDUP and hw.get(s_, -1) >= v:
                continue
            hw[s_] = v
            waits.append((s_, v))
        self.cnt[sem] += inc
        self.ops.append((eng, fn, sem, inc, waits))
        return (sem, self.cnt[sem])

    def barrier(self):
        return [(s_, v) for s_, v in self.cnt.items() if v > 0]

    def emit(self, nc, sems):
        by_eng = {e: [] for e in ENGS}
        for eng, fn, sem, inc, waits in self.ops:
            by_eng[eng].append((fn, sem, inc, waits))
        final = self.barrier()
        with nc.Block() as block:
            for e in ENGS:
                def make(ee):
                    def body(engine):
                        for fn, sem, inc, waits in by_eng[ee]:
                            for (s_, v) in waits:
                                engine.wait_ge(sems[s_], v)
                            fn(engine).then_inc(sems[sem], inc)
                        if ee == "sync":
                            for (s_, v) in final:
                                engine.wait_ge(sems[s_], v)
                    return body
                getattr(block, e)(make(e))


def build_program(dev=False):
    nc = bass.Bass("TRN2", target_bir_lowering=False)
    SCR = "ExternalOutput" if dev else "Internal"

    xd = nc.dram_tensor("x", [S, D], F32, kind="ExternalInput").ap()
    idxd = nc.dram_tensor("own_idx", [128, NOWN], U32, kind="ExternalInput").ap()
    sposd = nc.dram_tensor("own_pos", [128, NOWN], F32, kind="ExternalInput").ap()
    wspec = {
        "attn_qkv_w": [16, 48], "conv_all": [256], "attn_conv_b0": [8],
        "attn_conv_b1": [8], "attn_proj_w": [16, 16],
        "attn_proj_b": [16], "attn_ln_w": [16], "attn_ln_b": [16],
        "ffn_ln1_w": [D], "ffn_ln2_w": [D], "ffn_wq": [D, D], "ffn_wk": [D, 512],
        "ffn_wv": [D, 512], "ffn_wo": [D, D], "ffn_qnorm_w": [D],
        "ffn_knorm_w": [512], "ffn_router_w": [D, E], "ffn_router_bias": [E],
        "ffn_wg": [E, D, MI], "ffn_wu": [E, D, MI], "ffn_wd": [E, MI, D],
    }
    w = {k: nc.dram_tensor(k, shp, F32, kind="ExternalInput").ap() for k, shp in wspec.items()}
    outd = nc.dram_tensor("out", [NOWN * 128, D], F32, kind="ExternalOutput").ap()

    attd = nc.dram_tensor("att_scr", [S, D], BF16, kind=SCR).ap()
    x2d = nc.dram_tensor("x2_scr", [S, D], BF16, kind=SCR).ap()
    x1d = nc.dram_tensor("x1_scr", [NOWN * 128, D], BF16, kind=SCR).ap()
    h2td = nc.dram_tensor("h2t_scr", [16 * 128, NOWN * 128], BF16, kind=SCR).ap()
    otd = nc.dram_tensor("ot_scr", [16 * 128, NOWN * 128], BF16, kind=SCR).ap()
    dbgd = nc.dram_tensor("dbg_scr", [12, 128, D], BF16, kind=SCR).ap() if dev else None
    dbgf = nc.dram_tensor("dbgf_scr", [4, 128, 16], F32, kind=SCR).ap() if dev else None

    P = Plan()
    st = contextlib.ExitStack()
    sems = {s_: st.enter_context(nc.semaphore(s_)) for s_ in P.cnt}

    st1 = contextlib.ExitStack()
    stA = contextlib.ExitStack()
    stpc = contextlib.ExitStack()
    stt = contextlib.ExitStack()
    stq = contextlib.ExitStack()
    stat2 = contextlib.ExitStack()
    st2a = contextlib.ExitStack()
    stacc = contextlib.ExitStack()
    stmoe = contextlib.ExitStack()
    st2c = contextlib.ExitStack()
    stp4 = contextlib.ExitStack()

    def sb(name, shape, dt, stk=None):
        return (stk or st).enter_context(nc.sbuf_tensor(name, shape, dt))

    def ps(name, shape, dt):
        return st.enter_context(nc.psum_tensor(name, shape, dt))

    psA = ps("psA", [128, 2048], F32)
    psC = ps("psC", [128, 1024], F32)
    psX = ps("psX", [128, 1024], BF16)
    psY = ps("psY", [128, 1024], BF16)
    psT = [psX, psY]

    ident = sb("ident", [128, 128], BF16)
    i32a = sb("i32a", [128, 128], I32)
    onesc = sb("onesc", [128, 1], BF16)
    ones1 = sb("ones1", [1, 128], F32)
    rowbuf = sb("rowbuf", [1, 512], F32)
    cvst = sb("cvst", [1, 256], F32)
    statb = sb("statb", [128, 16], F32)
    junk = sb("junk", [128, D], BF16)
    SQb = sb("SQb", [128, D], BF16)
    hob = SQb
    x1b = sb("x1b", [128, D], BF16)
    h2b = sb("h2b", [128, D], BF16)

    t_c2 = P.op("sync", lambda e: e.dma_start(out=cvst[0:1, :], in_=w["conv_all"][None, :]), dma=True)
    t = P.op("gpsimd", lambda e: e.iota(i32a[:, 0:128], pattern=[[1, 128]], base=0, channel_multiplier=-1))
    t_id = P.op("vector", lambda e: e.tensor_scalar(out=ident[:, :], in0=i32a[:, 0:128], scalar1=0.0, scalar2=None, op0=ALU.is_equal), deps=[t])
    t_onesc = P.op("vector", lambda e: e.memset(onesc[:, :], 1.0))
    t_ones1 = P.op("vector", lambda e: e.memset(ones1[:, :], 1.0))

    last_b = [t_ones1]

    def bcast_row(dst_ap, src_row_ap, n):
        toks = []
        for b0 in range(0, n, 512):
            bw = min(512, n - b0)
            t_l = P.op("sync", lambda e, b0=b0, bw=bw: e.dma_start(
                out=rowbuf[0:1, 0:bw], in_=src_row_ap[:, b0:b0 + bw]), deps=last_b + toks, dma=True)
            for c0 in range(0, bw, 512):
                cw = min(512, bw - c0)
                t_m = P.op("tensor", lambda e, c0=c0, cw=cw: e.matmul(
                    psA[:, 0:cw], ones1[0:1, :], rowbuf[0:1, c0:c0 + cw], start=True, stop=True),
                    deps=[t_l, t_ones1] + toks)
                toks.append(P.op("vector", lambda e, b0=b0, c0=c0, cw=cw: e.tensor_copy(
                    out=dst_ap[:, b0 + c0:b0 + c0 + cw], in_=psA[:, 0:cw]), deps=[t_m]))
            last_b[:] = toks[-1:]
        return toks[-1]

    LN1W = sb("LN1W", [128, D], BF16)
    LN2W = sb("LN2W", [128, D], BF16)
    QNW = sb("QNW", [128, D], BF16)
    biasB = sb("biasB", [128, E], F32)
    iota8 = sb("iota8", [128, E], F32)
    KNWB = sb("KNWB", [128, 512], BF16)
    RW = sb("RW", [128, 16 * E], BF16)
    idxS = sb("idxS", [128, NOWN], U32)
    sposS = sb("sposS", [128, NOWN], F32)
    posb = sb("posb", [128, 1], F32)
    rtp = sb("rtp", [128, 128], F32)
    invfB = sb("invfB", [128, 32], F32)
    rsc = sb("rsc", [128, 32], F32)
    sqt = sb("sqt", [128, 32], F32)
    cqt = sb("cqt", [128, 32], F32)
    KT = sb("KT", [128, KV * S], BF16, stA)
    VS = sb("VS", [128, NT * 512], BF16, stA)
    xb0 = sb("xb0", [128, D], F32, stA)
    xb = [xb0, xb0]
    lnwB = sb("lnwB", [128, 16], F32, st1)
    lnbB = sb("lnbB", [128, 16], F32, st1)
    pbB = sb("pbB", [128, 16], F32, st1)
    cb1B = sb("cb1B", [128, 8], F32, st1)
    cb2B = sb("cb2B", [128, 8], F32, st1)
    cw1B = sb("cw1B", [128, 24], F32, st1)
    cw2B = sb("cw2B", [128, 24], F32, st1)

    if KSKIP >= 3:
        def bcast_row(*a, **k):
            return last_b[0]
    bcast_row(LN1W, w["ffn_ln1_w"][None, :], D)
    bcast_row(LN2W, w["ffn_ln2_w"][None, :], D)
    bcast_row(QNW, w["ffn_qnorm_w"][None, :], D)
    bcast_row(KNWB, w["ffn_knorm_w"][None, :], 512)
    bcast_row(lnwB, w["attn_ln_w"][None, :], 16)
    bcast_row(lnbB, w["attn_ln_b"][None, :], 16)
    bcast_row(pbB, w["attn_proj_b"][None, :], 16)
    bcast_row(biasB, w["ffn_router_bias"][None, :], E)
    bcast_row(cb1B, w["attn_conv_b0"][None, :], 8)
    bcast_row(cb2B, w["attn_conv_b1"][None, :], 8)
    tap_toks = []
    for dtap in range(3):
        for dst, seg, off in ((cw1B, 0, 3 + dtap), (cw2B, 128, 1 + 3 * dtap)):
            rhs = cvst[0:1, seg:seg + 72].rearrange("o (c n) -> o c n", n=9)[:, :, off]
            t_m = P.op("tensor", lambda e, rhs=rhs: e.matmul(
                psA[:, 0:8], ones1[0:1, :], rhs, start=True, stop=True),
                deps=[t_c2, t_ones1] + tap_toks)
            tap_toks.append(P.op("vector", lambda e, dst=dst, dtap=dtap: e.tensor_copy(
                out=dst[:, dtap * 8:dtap * 8 + 8], in_=psA[:, 0:8]), deps=[t_m]))
    last_b[:] = tap_toks[-1:]

    t = P.op("gpsimd", lambda e: e.iota(i32a[:, 0:8], pattern=[[1, 8]], base=0, channel_multiplier=0), deps=[t_id])
    P.op("vector", lambda e: e.tensor_copy(out=iota8[:, :], in_=i32a[:, 0:8]), deps=[t])

    Wq = sb("Wq", [128, 128], BF16, st1)
    Wk = sb("Wk", [128, 128], BF16, st1)
    Wv = sb("Wv", [128, 128], BF16, st1)
    Pbd = sb("Pbd", [128, 128], BF16, st1)
    t_z = [P.op("vector", lambda e, b_=b_: e.memset(b_[:, :], 0.0)) for b_ in (Wq, Wk, Wv, Pbd)]
    for gh in range(0 if KSKIP >= 1 else 8):
        for buf, sc in ((Wq, 0), (Wk, 16), (Wv, 32)):
            P.op("gpsimd", lambda e, buf=buf, gh=gh, sc=sc: e.dma_start(
                out=buf[16 * gh:16 * gh + 16, 16 * gh:16 * gh + 16],
                in_=w["attn_qkv_w"][:, sc:sc + 16]), deps=t_z, dma=True)
        P.op("gpsimd", lambda e, gh=gh: e.dma_start(
            out=Pbd[16 * gh:16 * gh + 16, 16 * gh:16 * gh + 16],
            in_=w["attn_proj_w"][:, :]), deps=t_z, dma=True)

    if KSKIP < 2:
        P.op("gpsimd", lambda e: e.dma_start(out=RW[:, :].rearrange("p (c f) -> p c f", c=16), in_=w["ffn_router_w"].rearrange("(c p) f -> p c f", p=128)), dma=True)

    P.op("sync", lambda e: e.dma_start(out=idxS[:, :], in_=idxd), dma=True)
    P.op("sync", lambda e: e.dma_start(out=sposS[:, :], in_=sposd), dma=True)

    CONST = P.barrier()
    P.fence(CONST)
    if os.environ.get("KMINI", "0") == "1":
        xbm = xb[0]
        snm = x1b
        dbgm = dbgd
        t_x = P.op("sync", lambda e: e.dma_start(out=xbm[:, :], in_=xd[0:128, :]), dma=True)
        t_s = P.op("vector", lambda e: e.tensor_copy(out=snm[:, :], in_=xbm[:, :]), deps=[t_x])
        P.op("sync", lambda e: e.dma_start(out=dbgm[5], in_=snm[:, :]), deps=[t_s], dma=True)
        P.emit(nc, sems)
        nc._keep = (st,)
        return nc



    t = P.op("gpsimd", lambda e: e.iota(i32a[:, 0:32], pattern=[[1, 32]], base=0, channel_multiplier=0), deps=[t_id])
    t = P.op("vector", lambda e: e.tensor_copy(out=invfB[:, :], in_=i32a[:, 0:32]), deps=[t])
    t_invB = P.op("scalar", lambda e: e.activation(out=invfB[:, :], in_=invfB[:, :], func=AF.Exp, scale=-float(np.log(THETA) / 32.0)), deps=[t])

    def rope_sin(dst, deps):
        t = P.op("vector", lambda e: e.tensor_scalar(out=rsc[:, :], in0=dst[:, :], scalar1=float(1.0 / (2 * np.pi)), scalar2=None, op0=ALU.mult), deps=deps)
        t = P.op("vector", lambda e: e.tensor_copy(out=i32a[:, 0:32], in_=rsc[:, :]), deps=[t])
        t = P.op("vector", lambda e: e.tensor_copy(out=rsc[:, :], in_=i32a[:, 0:32]), deps=[t])
        for cc in (TWO_PI_HI, TWO_PI_MID, TWO_PI_LO):
            t = P.op("vector", lambda e, cc=cc: e.scalar_tensor_tensor(out=dst[:, :], in0=rsc[:, :], scalar=-cc, in1=dst[:, :], op0=ALU.mult, op1=ALU.add), deps=[t])
        return P.op("scalar", lambda e: e.activation(out=dst[:, :], in_=dst[:, :], func=AF.Sin), deps=[t])

    def slot_tables(pos_scalar_ap, deps):
        t_a1 = P.op("vector", lambda e: e.tensor_scalar(out=sqt[:, :], in0=invfB[:, :], scalar1=pos_scalar_ap, scalar2=None, op0=ALU.mult), deps=deps + [t_invB])
        t_s = rope_sin(sqt, [t_a1])
        t_a2 = P.op("vector", lambda e: e.tensor_scalar(out=cqt[:, :], in0=invfB[:, :], scalar1=pos_scalar_ap, scalar2=float(np.pi / 2), op0=ALU.mult, op1=ALU.add), deps=[t_s])
        t_c = rope_sin(cqt, [t_a2])
        return t_s, t_c


    def rowstats(src_ap, n, eps_val, deps):
        t_s = P.op("vector", lambda e: e.tensor_reduce(out=statb[:, 0:1], in_=src_ap, axis=AX.X, op=ALU.add), deps=deps)
        if dev:
            t_s = P.op("vector", lambda e: e.tensor_copy(out=statb[:, 5:6], in_=statb[:, 0:1]), deps=[t_s])
        t_q = P.op("scalar", lambda e: e.activation(out=junk[:, 0:n], in_=src_ap, func=AF.Square, accum_out=statb[:, 1:2]), deps=deps + [t_s])
        t = P.op("vector", lambda e: e.tensor_scalar(out=statb[:, 0:1], in0=statb[:, 0:1], scalar1=float(1.0 / n), scalar2=None, op0=ALU.mult), deps=[t_s])
        t = P.op("vector", lambda e: e.tensor_scalar(out=statb[:, 1:2], in0=statb[:, 1:2], scalar1=float(1.0 / n), scalar2=None, op0=ALU.mult), deps=[t_q, t])
        t = P.op("vector", lambda e: e.tensor_tensor(out=statb[:, 2:3], in0=statb[:, 0:1], in1=statb[:, 0:1], op=ALU.mult), deps=[t])
        t = P.op("vector", lambda e: e.tensor_tensor(out=statb[:, 1:2], in0=statb[:, 1:2], in1=statb[:, 2:3], op=ALU.subtract), deps=[t])
        t = P.op("vector", lambda e, ev=eps_val: e.tensor_scalar(out=statb[:, 1:2], in0=statb[:, 1:2], scalar1=ev, scalar2=None, op0=ALU.add), deps=[t])
        t = P.op("scalar", lambda e: e.activation(out=statb[:, 1:2], in_=statb[:, 1:2], func=AF.Ln), deps=[t])
        t = P.op("scalar", lambda e: e.activation(out=statb[:, 3:4], in_=statb[:, 1:2], func=AF.Exp, scale=-0.5), deps=[t])
        t = P.op("vector", lambda e: e.tensor_tensor(out=statb[:, 4:5], in0=statb[:, 0:1], in1=statb[:, 3:4], op=ALU.mult), deps=[t])
        t = P.op("vector", lambda e: e.tensor_scalar(out=statb[:, 4:5], in0=statb[:, 4:5], scalar1=-1.0, scalar2=None, op0=ALU.mult), deps=[t])
        return t

    def rmsstats(src_ap, n, eps_val, deps, col=6):
        t_q = P.op("scalar", lambda e: e.activation(out=SQb[:, 0:n], in_=src_ap, func=AF.Square, accum_out=statb[:, col:col + 1]), deps=deps)
        t = P.op("vector", lambda e: e.tensor_scalar(out=statb[:, col:col + 1], in0=statb[:, col:col + 1], scalar1=float(1.0 / n), scalar2=eps_val, op0=ALU.mult, op1=ALU.add), deps=[t_q])
        t = P.op("scalar", lambda e: e.activation(out=statb[:, col:col + 1], in_=statb[:, col:col + 1], func=AF.Ln), deps=[t])
        t = P.op("scalar", lambda e: e.activation(out=statb[:, col:col + 1], in_=statb[:, col:col + 1], func=AF.Exp, scale=-0.5), deps=[t])
        return t

    yb = sb("yb", [128, D], BF16, st1)
    yT = sb("yT", [128, D], BF16, st1)
    Tq = sb("Tq", [128, D], BF16, st1)
    Tk = sb("Tk", [128, D], BF16, st1)
    Tv = sb("Tv", [128, D], BF16, st1)
    q2t = sb("q2t", [128, 1024], BF16, st1)
    k2t = sb("k2t", [128, 1024], BF16, st1)
    v2t = sb("v2t", [128, 1024], BF16, st1)
    OC = sb("OC", [128, D], BF16, st1)
    OC2 = sb("OC2", [128, 1024], BF16, st1)
    OCP = sb("OCP", [128, D], BF16, st1)
    NF = sb("NF", [128, D], BF16, st1)
    OTg = sb("OTg", [128, 128], BF16, st1)
    pr1 = sb("pr1", [128, 4096], BF16, st1)
    lv1 = sb("lv1", [128, 2048], BF16, st1)
    lv2 = sb("lv2", [128, 1024], BF16, st1)
    Sbuf = sb("Sbuf", [128, 512], F32, st1)
    mbuf = sb("mbuf", [128, 64], F32, st1)
    Ebuf = sb("Ebuf", [128, 512], BF16, st1)
    dbuf = sb("dbuf", [128, 64], F32, st1)
    rbuf = sb("rbuf", [128, 64], F32, st1)
    Enb = sb("Enb", [128, 512], BF16, st1)
    pr2 = sb("pr2", [128, 8 * 8 * 12], BF16, st1)
    mug = sb("mug", [128, 128], F32, st1)
    msg = sb("msg", [128, 128], F32, st1)
    rsg = sb("rsg", [128, 128], F32, st1)
    m2g = sb("m2g", [128, 128], F32, st1)

    def branch(nW, qv, kv_, vv, cwB, cbB, dst_fn, deps):
        toks = list(deps)
        aw = nW * 8
        tprod = []
        for a in range(8):
            tprod.append(P.op("vector", lambda e, a=a: e.tensor_tensor(
                out=pr1[:, a * nW * 64:(a + 1) * nW * 64].rearrange("p (w b c) -> p w b c", b=8, c=8),
                in0=qv(a), in1=kv_(a), op=ALU.mult), deps=toks))
        t = P.op("vector", lambda e: e.tensor_tensor(
            out=lv1[:, 0:aw * 32].rearrange("p (x c) -> p x c", c=4),
            in0=pr1[:, 0:aw * 64].rearrange("p (x c) -> p x c", c=8)[:, :, 0:4],
            in1=pr1[:, 0:aw * 64].rearrange("p (x c) -> p x c", c=8)[:, :, 4:8],
            op=ALU.add), deps=tprod)
        t = P.op("vector", lambda e: e.tensor_tensor(
            out=lv2[:, 0:aw * 16].rearrange("p (x c) -> p x c", c=2),
            in0=lv1[:, 0:aw * 32].rearrange("p (x c) -> p x c", c=4)[:, :, 0:2],
            in1=lv1[:, 0:aw * 32].rearrange("p (x c) -> p x c", c=4)[:, :, 2:4],
            op=ALU.add), deps=[t])
        t = P.op("vector", lambda e: e.tensor_tensor(
            out=Sbuf[:, 0:aw * 8],
            in0=lv2[:, 0:aw * 16].rearrange("p (x c) -> p x c", c=2)[:, :, 0:1],
            in1=lv2[:, 0:aw * 16].rearrange("p (x c) -> p x c", c=2)[:, :, 1:2],
            op=ALU.add), deps=[t])
        t = P.op("vector", lambda e: e.tensor_reduce(
            out=mbuf[:, 0:aw], in_=Sbuf[:, 0:aw * 8].rearrange("p (x b) -> p x b", b=8),
            axis=AX.X, op=ALU.max), deps=[t])
        t = P.op("vector", lambda e: e.tensor_tensor(
            out=Sbuf[:, 0:aw * 8].rearrange("p (x b) -> p x b", b=8),
            in0=Sbuf[:, 0:aw * 8].rearrange("p (x b) -> p x b", b=8),
            in1=mbuf[:, 0:aw][:, :, None].broadcast_to([128, aw, 8]),
            op=ALU.subtract), deps=[t])
        t = P.op("scalar", lambda e: e.activation(out=Ebuf[:, 0:aw * 8], in_=Sbuf[:, 0:aw * 8], func=AF.Exp, scale=SCALE_B), deps=[t])
        t = P.op("vector", lambda e: e.tensor_reduce(
            out=dbuf[:, 0:aw], in_=Ebuf[:, 0:aw * 8].rearrange("p (x b) -> p x b", b=8),
            axis=AX.X, op=ALU.add), deps=[t])
        t = P.op("vector", lambda e: e.reciprocal(out=rbuf[:, 0:aw], in_=dbuf[:, 0:aw]), deps=[t])
        t_en = P.op("vector", lambda e: e.tensor_tensor(
            out=Enb[:, 0:aw * 8].rearrange("p (x b) -> p x b", b=8),
            in0=Ebuf[:, 0:aw * 8].rearrange("p (x b) -> p x b", b=8),
            in1=rbuf[:, 0:aw][:, :, None].broadcast_to([128, aw, 8]),
            op=ALU.mult), deps=[t])
        outs = []
        for a in range(8):
            t2 = P.op("vector", lambda e, a=a: e.tensor_tensor(
                out=pr2[:, 0:nW * 96].rearrange("p (w c b) -> p w c b", c=8, b=12)[:, :, :, 0:8],
                in0=Enb[:, a * nW * 8:(a + 1) * nW * 8].rearrange("p (w b) -> p w b", b=8)[:, :, None, :].broadcast_to([128, nW, 8, 8]),
                in1=vv(a), op=ALU.mult), deps=[t_en] + outs[-1:])
            tl = [t2]
            for dt in range(3):
                av = a + dt - 1
                if av < 0 or av > 7:
                    tl.append(P.op("vector", lambda e, dt=dt: e.memset(
                        pr2[:, 0:nW * 96].rearrange("p (w c b) -> p w c b", c=8, b=12)[:, :, :, 8 + dt:9 + dt], 0.0), deps=[t2]))
                else:
                    tl.append(P.op("vector", lambda e, dt=dt, av=av: e.tensor_tensor(
                        out=pr2[:, 0:nW * 96].rearrange("p (w c b) -> p w c b", c=8, b=12)[:, :, :, 8 + dt:9 + dt],
                        in0=vv(av, True),
                        in1=cwB[:, dt * 8:dt * 8 + 8][:, None, :, None].broadcast_to([128, nW, 8, 1]),
                        op=ALU.mult), deps=[t2]))
            outs.append(P.op("vector", lambda e, a=a: e.tensor_reduce(
                out=dst_fn(a), in_=pr2[:, 0:nW * 96].rearrange("p (w c b) -> p w c b", c=8, b=12),
                axis=AX.X, op=ALU.add), deps=tl))
        return outs

    def fill_cb(cbB, nW, deps):
        return P.op("vector", lambda e: e.tensor_scalar(
            out=pr2[:, 0:nW * 96].rearrange("p (w c b) -> p w c b", c=8, b=12)[:, :, :, 11:12],
            in0=cbB[:, None, :, None].broadcast_to([128, nW, 8, 1]),
            scalar1=0.0, scalar2=None, op0=ALU.add), deps=deps)

    scatter_toks = {0: [], 1: []}
    prev_sb_done = CONST

    for ub in range(KTRUNC if KTRUNC else NT):
        r0 = 128 * ub
        pb = ub % 2
        t_x = P.op("sync", lambda e, r0=r0, pb=pb: e.dma_start(out=xb[pb][:, :], in_=xd[r0:r0 + 128, :]), deps=prev_sb_done, dma=True)
        if dev:
            for _pad in range(int(os.environ.get("KPAD", "1"))):
                t_x = P.op("sync", lambda e, r0=r0, pb=pb: e.dma_start(out=xb[pb][:, :], in_=xd[r0:r0 + 128, :]), deps=[t_x], dma=True)
        if dev and ub == 0:
            t_snap = P.op("vector", lambda e, pb=pb: e.tensor_copy(out=x1b[:, :], in_=xb[pb][:, :]), deps=[t_x])
            P.op("sync", lambda e: e.dma_start(out=dbgd[8], in_=x1b[:, :]), deps=[t_snap], dma=True)
        if dev and ub == 0 and os.environ.get("KCUT", "0") == "1":
            with nc.allow_low_precision(reason="dev"):
                P.emit(nc, sems)
            nc._keep = (st,)
            return nc
        t_st = rowstats(xb[pb][:, :], D, EPS_LN, [t_x])
        if dev and ub == 0:
            P.op("sync", lambda e: e.dma_start(out=dbgf[0], in_=statb[:, :]), deps=[t_st], dma=True)
        t_y = P.op("vector", lambda e, pb=pb: e.tensor_scalar(out=yb[:, :], in0=xb[pb][:, :], scalar1=statb[:, 3:4], scalar2=statb[:, 4:5], op0=ALU.mult, op1=ALU.add), deps=[t_st])
        if dev and ub == 0:
            P.op("sync", lambda e: e.dma_start(out=dbgf[1], in_=statb[:, :]), deps=[t_y], dma=True)
            P.op("sync", lambda e: e.dma_start(out=dbgd[11], in_=yb[:, :]), deps=[t_y], dma=True)
        evac_toks = []
        prev_evac = {0: [t_y], 1: [t_y]}
        for g4 in range(4):
            pt = g4 % 2
            trs = []
            for k4 in range(4):
                ch = g4 * 4 + k4
                trs.append(P.op("tensor", lambda e, pt=pt, k4=k4, ch=ch: e.transpose(
                    psT[pt][:, 128 * k4:128 * k4 + 128], yb[:, 128 * ch:128 * ch + 128], ident[:, :]),
                    deps=[t_y, t_id] + prev_evac[pt]))
            tev = P.op("vector", lambda e, pt=pt, g4=g4: e.tensor_copy(out=yT[:, 512 * g4:512 * g4 + 512], in_=psT[pt][:, 0:512]), deps=trs)
            prev_evac[pt] = [tev]
            evac_toks.append(tev)
        tqkv = []
        prev_q_evac = [evac_toks[-1]]
        for g4 in range(4):
            mm_toks = []
            for k4 in range(4):
                G = g4 * 4 + k4
                for si, Wx in ((0, Wq), (1, Wk), (2, Wv)):
                    mm_toks.append(P.op("tensor", lambda e, si=si, Wx=Wx, G=G, k4=k4: e.matmul(
                        psA[:, 512 * si + 128 * k4:512 * si + 128 * k4 + 128],
                        yT[:, 128 * G:128 * G + 128], Wx[:, :],
                        start=True, stop=True), deps=[evac_toks[G // 4]] + prev_q_evac))
            ev = []
            for si, Tx in ((0, Tq), (1, Tk), (2, Tv)):
                ev.append(P.op("vector", lambda e, si=si, Tx=Tx, g4=g4: e.tensor_copy(
                    out=Tx[:, 512 * g4:512 * g4 + 512], in_=psA[:, 512 * si:512 * si + 512]), deps=mm_toks))
            prev_q_evac = ev
            tqkv += ev
        b2toks = []
        prev_pt = {0: tqkv, 1: tqkv}
        pi = 0
        for Tx, dstt in ((Tq, q2t), (Tk, k2t), (Tv, v2t)):
            for ci in range(8):
                o = 8 + ci
                pt = pi % 2
                ttr = P.op("tensor", lambda e, Tx=Tx, o=o, pt=pt: e.transpose(
                    psT[pt][:, 0:128],
                    Tx[:, :].rearrange("p (x o) -> p o x", o=16)[:, o, :],
                    ident[:, :]), deps=tqkv + [t_id] + prev_pt[pt])
                tev = P.op("vector", lambda e, dstt=dstt, ci=ci, pt=pt: e.tensor_copy(
                    out=dstt[:, :].rearrange("g (b s p c) -> g b p s c", b=4, s=4, p=8)[:, :, :, :, ci],
                    in_=psT[pt][:, 0:128].rearrange("g (b p s) -> g b p s", b=4, p=8)), deps=[ttr])
                prev_pt[pt] = [tev]
                b2toks.append(tev)
                pi += 1
        tcb = fill_cb(cb1B, 4, tqkv)
        b1outs = []
        for bat in range(4):
            G0 = bat * 4

            def qv1(a, G0=G0):
                return Tq[:, :].rearrange("p (G h o) -> p G h o", h=8, o=16)[:, G0:G0 + 4, a, 0:8][:, :, None, :].broadcast_to([128, 4, 8, 8])

            def kv1(a, G0=G0):
                return Tk[:, :].rearrange("p (G h o) -> p G h o", h=8, o=16)[:, G0:G0 + 4, :, 0:8]

            def vv1(a, single=False, G0=G0):
                v = Tv[:, :].rearrange("p (G h o) -> p G o h", h=8, o=16)[:, G0:G0 + 4, 0:8, :]
                if single:
                    return v[:, :, :, a:a + 1]
                return v

            def dst1(a, G0=G0):
                return OC[:, :].rearrange("p (g c) -> p g c", c=16).rearrange("p (G h) c -> p G h c", h=8)[:, G0:G0 + 4, a, 0:8]

            b1outs += branch(4, qv1, kv1, vv1, cw1B, cb1B, dst1,
                             tqkv + [tcb] + b1outs[-1:])
        tcb2 = fill_cb(cb2B, 8, b1outs)
        b2outs = []
        for bat in range(2):
            B0 = bat * 2

            def qv2(a, B0=B0):
                return q2t[:, :].rearrange("g (b s p c) -> g b s p c", b=4, s=4, p=8)[:, B0:B0 + 2, :, a, :].rearrange("g b s c -> g (b s) c")[:, :, None, :].broadcast_to([128, 8, 8, 8])

            def kv2(a, B0=B0):
                return k2t[:, :].rearrange("g (b s p c) -> g b s p c", b=4, s=4, p=8)[:, B0:B0 + 2, :, :, :].rearrange("g b s p c -> g (b s) p c")

            def vv2(a, single=False, B0=B0):
                v = v2t[:, :].rearrange("g (b s p c) -> g b s c p", b=4, s=4, p=8)[:, B0:B0 + 2, :, :, :].rearrange("g b s c p -> g (b s) c p")
                if single:
                    return v[:, :, :, a:a + 1]
                return v

            def dst2(a, B0=B0):
                return OC2[:, :].rearrange("g (b p s c) -> g b p s c", b=4, p=8, s=4)[:, B0:B0 + 2, a, :, :]

            b2outs += branch(8, qv2, kv2, vv2, cw2B, cb2B, dst2,
                             b2toks + [tcb2] + b2outs[-1:])
        prev_bt = {0: b2outs, 1: b2outs}
        btoks = []
        for ci in range(8):
            pt = ci % 2
            ttr = P.op("tensor", lambda e, ci=ci, pt=pt: e.transpose(
                psT[pt][:, 0:128],
                OC2[:, :].rearrange("g (r c) -> g c r", c=8)[:, ci, :],
                ident[:, :]), deps=b2outs + [t_id] + prev_bt[pt])
            tev = P.op("vector", lambda e, ci=ci, pt=pt: e.tensor_copy(
                out=OC[:, :].rearrange("p (g c) -> p c g", c=16)[:, 8 + ci, :], in_=psT[pt][:, 0:128]), deps=[ttr])
            prev_bt[pt] = [tev]
            btoks.append(tev)
        oc_done = b1outs + btoks
        proj_toks = []
        prev_ot = oc_done
        for G in range(16):
            ttr = P.op("tensor", lambda e, G=G: e.transpose(
                psT[0][:, 0:128],
                OC[:, 128 * G:128 * G + 128],
                ident[:, :]), deps=prev_ot + [t_id])
            tev = P.op("vector", lambda e: e.tensor_copy(out=OTg[:, :], in_=psT[0][:, 0:128]), deps=[ttr])
            tmm = P.op("tensor", lambda e, G=G: e.matmul(
                psA[:, 128 * G:128 * G + 128],
                OTg[:, :], Pbd[:, :], start=True, stop=True), deps=[tev])
            prev_ot = [tmm]
            proj_toks.append(tmm)
        evo = []
        for o in range(16):
            evo.append(P.op("vector", lambda e, o=o: e.tensor_scalar(
                out=OCP[:, o * 128:(o + 1) * 128],
                in0=psA[:, :].rearrange("p (G h o) -> p o G h", h=8, o=16)[:, o, :, :],
                scalar1=pbB[:, o:o + 1], scalar2=None, op0=ALU.add), deps=proj_toks))
        t_mu = P.op("vector", lambda e: e.tensor_reduce(
            out=mug[:, :], in_=OCP[:, :].rearrange("p (c g) -> p g c", c=16), axis=AX.X, op=ALU.add), deps=evo)
        t_sq = P.op("scalar", lambda e: e.activation(out=SQb[:, :], in_=OCP[:, :], func=AF.Square), deps=evo)
        t_ms = P.op("vector", lambda e: e.tensor_reduce(
            out=msg[:, :], in_=SQb[:, :].rearrange("p (c g) -> p g c", c=16), axis=AX.X, op=ALU.add), deps=[t_sq])
        t = P.op("vector", lambda e: e.tensor_scalar(out=mug[:, :], in0=mug[:, :], scalar1=float(1.0 / 16), scalar2=None, op0=ALU.mult), deps=[t_mu])
        t = P.op("vector", lambda e: e.tensor_scalar(out=msg[:, :], in0=msg[:, :], scalar1=float(1.0 / 16), scalar2=None, op0=ALU.mult), deps=[t_ms, t])
        t = P.op("vector", lambda e: e.tensor_tensor(out=rsg[:, :], in0=mug[:, :], in1=mug[:, :], op=ALU.mult), deps=[t])
        t = P.op("vector", lambda e: e.tensor_tensor(out=rsg[:, :], in0=msg[:, :], in1=rsg[:, :], op=ALU.subtract), deps=[t])
        t = P.op("vector", lambda e: e.tensor_scalar(out=rsg[:, :], in0=rsg[:, :], scalar1=EPS_LN, scalar2=None, op0=ALU.add), deps=[t])
        t = P.op("scalar", lambda e: e.activation(out=rsg[:, :], in_=rsg[:, :], func=AF.Ln), deps=[t])
        t = P.op("scalar", lambda e: e.activation(out=rsg[:, :], in_=rsg[:, :], func=AF.Exp, scale=-0.5), deps=[t])
        t = P.op("vector", lambda e: e.tensor_tensor(out=m2g[:, :], in0=mug[:, :], in1=rsg[:, :], op=ALU.mult), deps=[t])
        t_ln = P.op("vector", lambda e: e.tensor_scalar(out=m2g[:, :], in0=m2g[:, :], scalar1=-1.0, scalar2=None, op0=ALU.mult), deps=[t])
        nf_toks = []
        for c in range(16):
            t1 = P.op("vector", lambda e, c=c: e.scalar_tensor_tensor(
                out=NF[:, c * 128:(c + 1) * 128], in0=OCP[:, c * 128:(c + 1) * 128],
                scalar=lnwB[:, c:c + 1], in1=rsg[:, :], op0=ALU.mult, op1=ALU.mult), deps=[t_ln])
            t2 = P.op("vector", lambda e, c=c: e.scalar_tensor_tensor(
                out=junk[:, 0:128], in0=m2g[:, :], scalar=lnwB[:, c:c + 1],
                in1=NF[:, c * 128:(c + 1) * 128], op0=ALU.mult, op1=ALU.add), deps=[t1])
            t3 = P.op("vector", lambda e, c=c: e.scalar_tensor_tensor(
                out=NF[:, c * 128:(c + 1) * 128],
                in0=yb[:, :].rearrange("p (g c) -> p c g", c=16)[:, c, :],
                scalar=lnbB[:, c:c + 1], in1=junk[:, 0:128], op0=ALU.add, op1=ALU.add), deps=[t2])
            nf_toks.append(t3)
        if dev and ub == 0:
            t_sn2 = P.op("vector", lambda e, pb=pb: e.tensor_copy(out=h2b[:, :], in_=xb[pb][:, :]), deps=nf_toks)
            P.op("sync", lambda e: e.dma_start(out=dbgd[9], in_=h2b[:, :]), deps=[t_sn2], dma=True)
        if dev and ub == 0:
            for di, buf in enumerate((yb, yT, Tq, Tk, OC, OCP, NF)):
                P.op("sync", lambda e, di=di, buf=buf: e.dma_start(out=dbgd[di], in_=buf[:, :]), deps=nf_toks, dma=True)
            P.op("sync", lambda e: e.dma_start(out=dbgd[7][:, 0:1024], in_=OC2[:, :]), deps=nf_toks, dma=True)
        m_base = 8 * ub
        sc_toks = []
        for beta in range(4):
            for phi in range(2):
                mval = m_base + 2 * beta + phi
                sc_toks.append(P.op("sync", lambda e, beta=beta, phi=phi, mval=mval: e.dma_start(
                    out=attd.rearrange("(c m) (t g) -> m t c g", m=256, g=128)[mval],
                    in_=NF[32 * beta + 16 * phi:32 * beta + 16 * phi + 16, :].rearrange("p (c g) -> p c g", c=16)),
                    deps=nf_toks, dma=True))
        scatter_toks[ub // 16] += sc_toks
        prev_sb_done = nf_toks + sc_toks

    P1_DONE = P.barrier()
    st1.close()
    if KTRUNC:
        with nc.allow_low_precision(reason="bf16 stream"):
            P.emit(nc, sems)
        nc._keep = (st, stA)
        return nc

    WKsb = sb("WKsb", [128, 16 * 512], BF16, stpc)
    WVsb = sb("WVsb", [128, 16 * 512], BF16, stpc)

    attc = sb("attc", [128, D], BF16, stpc)
    x2t = sb("x2t", [128, D], BF16, stpc)
    ht = sb("ht", [128, D], BF16, stpc)
    hT = sb("hT", [128, D], BF16, stpc)
    ktmp = sb("ktmp", [128, 512], BF16, stpc)
    ssr = rowbuf

    P.fence(P1_DONE)
    twv1 = P.op("gpsimd", lambda e: e.dma_start(out=WKsb[:, :].rearrange("p (c f) -> p c f", c=16), in_=w["ffn_wk"].rearrange("(c p) f -> p c f", p=128)), deps=P1_DONE, dma=True)
    twv2 = P.op("gpsimd", lambda e: e.dma_start(out=WVsb[:, :].rearrange("p (c f) -> p c f", c=16), in_=w["ffn_wv"].rearrange("(c p) f -> p c f", p=128)), deps=P1_DONE, dma=True)
    pc_prev = P1_DONE
    kv_done = []
    for T in range(NT):
        r0 = 128 * T
        half = T % 2
        dep0 = pc_prev + scatter_toks[half]
        t_a = P.op("sync", lambda e, r0=r0: e.dma_start(out=attc[:, :], in_=attd[r0:r0 + 128, :]), deps=dep0, dma=True)
        t_x = P.op("sync", lambda e, r0=r0: e.dma_start(out=xb[0][:, :], in_=xd[r0:r0 + 128, :]), deps=dep0, dma=True)
        t_st = rowstats(attc[:, :], D, EPS_LN, [t_a])
        t1 = P.op("vector", lambda e: e.tensor_scalar(out=junk[:, :], in0=attc[:, :], scalar1=statb[:, 3:4], scalar2=statb[:, 4:5], op0=ALU.mult, op1=ALU.add), deps=[t_st])
        t_x2 = P.op("vector", lambda e: e.tensor_tensor(out=x2t[:, :], in0=junk[:, :], in1=xb[0][:, :], op=ALU.add), deps=[t1, t_x])
        t_sp = P.op("sync", lambda e, r0=r0: e.dma_start(out=x2d[r0:r0 + 128, :], in_=x2t[:, :]), deps=[t_x2], dma=True)
        t_st2 = rowstats(x2t[:, :], D, EPS_LN, [t_x2])
        t2 = P.op("vector", lambda e: e.tensor_scalar(out=junk[:, :], in0=x2t[:, :], scalar1=statb[:, 3:4], scalar2=statb[:, 4:5], op0=ALU.mult, op1=ALU.add), deps=[t_st2])
        t_rs = rmsstats(junk[:, :], D, EPS_RMS, [t2])
        t3 = P.op("vector", lambda e: e.tensor_scalar(out=ht[:, :], in0=junk[:, :], scalar1=statb[:, 6:7], scalar2=None, op0=ALU.mult), deps=[t_rs])
        t_h = P.op("vector", lambda e: e.tensor_tensor(out=ht[:, :], in0=ht[:, :], in1=LN1W[:, :], op=ALU.mult), deps=[t3])
        evh = []
        prev_e = {0: [t_h], 1: [t_h]}
        for g4 in range(4):
            pt = g4 % 2
            trs = []
            for k4 in range(4):
                ch = g4 * 4 + k4
                trs.append(P.op("tensor", lambda e, pt=pt, k4=k4, ch=ch: e.transpose(
                    psT[pt][:, 128 * k4:128 * k4 + 128], ht[:, 128 * ch:128 * ch + 128], ident[:, :]),
                    deps=[t_h, t_id] + prev_e[pt]))
            tev = P.op("vector", lambda e, pt=pt, g4=g4: e.tensor_copy(out=hT[:, 512 * g4:512 * g4 + 512], in_=psT[pt][:, 0:512]), deps=trs)
            prev_e[pt] = [tev]
            evh.append(tev)
        kmms = []
        for ch in range(16):
            kmms.append(P.op("tensor", lambda e, ch=ch: e.matmul(
                psA[:, 0:512], hT[:, 128 * ch:128 * ch + 128],
                WKsb[:, ch * 512:(ch + 1) * 512],
                start=(ch == 0), stop=(ch == 15)), deps=[evh[ch // 4]] + kv_done))
        vmms = []
        for ch in range(16):
            vmms.append(P.op("tensor", lambda e, ch=ch: e.matmul(
                psA[:, 512:1024], hT[:, 128 * ch:128 * ch + 128],
                WVsb[:, ch * 512:(ch + 1) * 512],
                start=(ch == 0), stop=(ch == 15)), deps=[evh[ch // 4]] + kv_done))
        t_rs2 = rmsstats(psA[:, 0:512], 512, EPS_RMS, kmms, col=7)
        t_ke = P.op("vector", lambda e: e.tensor_scalar(out=ktmp[:, :], in0=psA[:, 0:512], scalar1=statb[:, 7:8], scalar2=None, op0=ALU.mult), deps=[t_rs2] + kmms)
        t_kn = P.op("vector", lambda e: e.tensor_tensor(out=ktmp[:, :], in0=ktmp[:, :], in1=KNWB[:, :], op=ALU.mult), deps=[t_ke])
        t_pi = P.op("gpsimd", lambda e, r0=r0: e.iota(i32a[:, 33:34], pattern=[[0, 1]], base=r0, channel_multiplier=1), deps=[t_kn] + kv_done)
        t_pb = P.op("vector", lambda e: e.tensor_copy(out=posb[:, :], in_=i32a[:, 33:34]), deps=[t_pi])
        t_s, t_c = slot_tables(posb[:, :], [t_pb])
        rp = [t_s, t_c, t_kn]
        for kvh in range(4):
            c0 = 128 * kvh
            tA = P.op("vector", lambda e, c0=c0: e.tensor_tensor(out=rtp[:, 0:32], in0=ktmp[:, c0:c0 + 32], in1=cqt[:, :], op=ALU.mult), deps=rp)
            tB = P.op("vector", lambda e, c0=c0: e.tensor_tensor(out=rtp[:, 32:64], in0=ktmp[:, c0 + 32:c0 + 64], in1=sqt[:, :], op=ALU.mult), deps=rp)
            tC = P.op("vector", lambda e, c0=c0: e.tensor_tensor(out=rtp[:, 64:96], in0=ktmp[:, c0 + 32:c0 + 64], in1=cqt[:, :], op=ALU.mult), deps=rp)
            tD = P.op("vector", lambda e, c0=c0: e.tensor_tensor(out=rtp[:, 96:128], in0=ktmp[:, c0:c0 + 32], in1=sqt[:, :], op=ALU.mult), deps=rp)
            t1 = P.op("vector", lambda e, c0=c0: e.tensor_tensor(out=ktmp[:, c0:c0 + 32], in0=rtp[:, 0:32], in1=rtp[:, 32:64], op=ALU.subtract), deps=[tA, tB, tC, tD])
            rp = [P.op("vector", lambda e, c0=c0: e.tensor_tensor(out=ktmp[:, c0 + 32:c0 + 64], in0=rtp[:, 64:96], in1=rtp[:, 96:128], op=ALU.add), deps=[t1]), t_s, t_c]
        pt = T % 2
        ktr = []
        for kvh in range(4):
            ktr.append(P.op("tensor", lambda e, kvh=kvh, pt=pt: e.transpose(
                psT[pt][:, 128 * kvh:128 * kvh + 128], ktmp[:, 128 * kvh:128 * kvh + 128], ident[:, :]),
                deps=rp + [t_id] + kv_done))
        kst = []
        for kvh in range(4):
            kst.append(P.op("vector", lambda e, kvh=kvh, pt=pt, r0=r0: e.tensor_copy(
                out=KT[:, kvh * S + r0:kvh * S + r0 + 128], in_=psT[pt][:, 128 * kvh:128 * kvh + 128]), deps=ktr))
        t_ks = kst[-1]
        t_vs = P.op("vector", lambda e, T=T: e.tensor_copy(out=VS[:, 512 * T:512 * T + 512], in_=psA[:, 512:1024]), deps=vmms)
        kv_done = kst + [t_vs, t_ke]
        pc_prev = [t_ks, t_vs, t_sp, t_x2]

    PC_DONE = P.barrier()
    stpc.close()
    P.fence(PC_DONE)

    qT = sb("qT", [128, 16 * 16 * 128], BF16, stq)
    bndS = sb("bndS", [128, 16 * 16], F32, stat2)
    MnegS = sb("MnegS", [128, 16 * 256], BF16, stat2)
    KmxB = sb("KmxB", [128, 1], F32, stat2)
    Astr = sb("Astr", [128, 32 * 128], BF16, stat2)
    ATb = [sb("ATb0", [128, 128], BF16, stat2), sb("ATb1", [128, 128], BF16, stat2)]
    dnmb = sb("dnmb", [128, 8], F32, stat2)
    rcpb = sb("rcpb", [128, 1], F32, stat2)
    otb = sb("otb", [128, 128], BF16, stat2)

    hoT = [sb("hoT%d" % k, [128, D], BF16, st2a) for k in range(2)]
    qrawS = [sb("qrawS%d" % k, [128, D], BF16, st2a) for k in range(2)]
    qhn = sb("qhn", [128, 16], F32, st2a)
    wqb = [sb("wqb0", [128, 512], BF16, st2a), sb("wqb1", [128, 512], BF16, st2a)]
    qsb = sb("qsb", [128, 16], F32, st2a)

    t_kq = P.op("scalar", lambda e: e.activation(out=junk[:, 0:2048], in_=KT[:, 0:2048], func=AF.Square), deps=PC_DONE)
    tmx = []
    t_mxacc = None
    for blk in range(KV * S // 2048):
        t_kq2 = P.op("scalar", lambda e, blk=blk: e.activation(out=junk[:, 0:2048], in_=KT[:, 2048 * blk:2048 * blk + 2048], func=AF.Square), deps=PC_DONE + tmx) if blk > 0 else t_kq
        t_m1 = P.op("tensor", lambda e: e.matmul(psC[0:1, 0:512], onesc[:, :], junk[:, 0:512], start=True, stop=True), deps=[t_kq2, t_onesc] + tmx)
        toksb = [t_m1]
        for sub in range(1, 4):
            toksb.append(P.op("tensor", lambda e, sub=sub: e.matmul(
                psC[0:1, 512:1024], onesc[:, :], junk[:, 512 * sub:512 * sub + 512],
                start=(sub == 1), stop=(sub == 3)), deps=[t_kq2] + toksb))
        t_r1 = P.op("vector", lambda e: e.tensor_reduce(out=ssr[0:1, 0:1], in_=psC[0:1, 0:512], axis=AX.X, op=ALU.max), deps=toksb)
        t_r2 = P.op("vector", lambda e: e.tensor_reduce(out=ssr[0:1, 1:2], in_=psC[0:1, 512:1024], axis=AX.X, op=ALU.max), deps=[t_r1])
        if t_mxacc is None:
            t_mxacc = P.op("vector", lambda e: e.tensor_reduce(out=ssr[0:1, 4:5], in_=ssr[0:1, 0:2], axis=AX.X, op=ALU.max), deps=[t_r2])
        else:
            t_m3 = P.op("vector", lambda e: e.tensor_reduce(out=ssr[0:1, 2:3], in_=ssr[0:1, 0:2], axis=AX.X, op=ALU.max), deps=[t_r2])
            t_mxacc = P.op("vector", lambda e: e.tensor_tensor(out=ssr[0:1, 4:5], in0=ssr[0:1, 4:5], in1=ssr[0:1, 2:3], op=ALU.max), deps=[t_m3, t_mxacc])
        tmx = [t_mxacc]
    t = P.op("scalar", lambda e: e.activation(out=ssr[0:1, 4:5], in_=ssr[0:1, 4:5], func=AF.Sqrt), deps=tmx)
    t = P.op("vector", lambda e: e.tensor_scalar(out=ssr[0:1, 4:5], in0=ssr[0:1, 4:5], scalar1=-SCALE_F, scalar2=None, op0=ALU.mult), deps=[t])
    t = P.op("tensor", lambda e: e.matmul(psC[:, 512:513], ones1[0:1, :], ssr[0:1, 4:5], start=True, stop=True), deps=[t, t_ones1])
    t_kmx = P.op("vector", lambda e: e.tensor_copy(out=KmxB[:, :], in_=psC[:, 512:513]), deps=[t])

    q_done = []
    for grp in range(8):
        gtoks = []
        for sl in range(2):
            i = grp * 2 + sl
            t_g = P.op("gpsimd", lambda e, i=i: e.indirect_dma_start(
                out=hob[:, :], out_offset=None, in_=x2d,
                in_offset=bass.IndirectOffsetOnAxis(idxS[:, i:i + 1], 0)),
                deps=PC_DONE + q_done + gtoks, dma=True)
            t_st = rowstats(hob[:, :], D, EPS_LN, [t_g])
            t1 = P.op("vector", lambda e, sl=sl: e.tensor_scalar(out=qrawS[sl][:, :], in0=hob[:, :], scalar1=statb[:, 3:4], scalar2=statb[:, 4:5], op0=ALU.mult, op1=ALU.add), deps=[t_st])
            t_rs = rmsstats(qrawS[sl][:, :], D, EPS_RMS, [t1])
            t2 = P.op("vector", lambda e, sl=sl: e.tensor_scalar(out=qrawS[sl][:, :], in0=qrawS[sl][:, :], scalar1=statb[:, 6:7], scalar2=None, op0=ALU.mult), deps=[t_rs])
            t_h = P.op("vector", lambda e, sl=sl: e.tensor_tensor(out=hob[:, :], in0=qrawS[sl][:, :], in1=LN1W[:, :], op=ALU.mult), deps=[t2])
            prev_e = {0: [t_h], 1: [t_h]}
            for g4 in range(4):
                pt = g4 % 2
                trs = []
                for k4 in range(4):
                    ch = g4 * 4 + k4
                    trs.append(P.op("tensor", lambda e, pt=pt, k4=k4, ch=ch: e.transpose(
                        psT[pt][:, 128 * k4:128 * k4 + 128], hob[:, 128 * ch:128 * ch + 128], ident[:, :]),
                        deps=[t_h, t_id] + prev_e[pt]))
                tev = P.op("vector", lambda e, pt=pt, g4=g4, sl=sl: e.tensor_copy(out=hoT[sl][:, 512 * g4:512 * g4 + 512], in_=psT[pt][:, 0:512]), deps=trs)
                prev_e[pt] = [tev]
                gtoks.append(tev)
            cap = CAPS[i]
            t_mi = P.op("gpsimd", lambda e, cap=cap: e.iota(i32a[:, 0:128], pattern=[[1, 128]], base=128 * (cap - 2), channel_multiplier=0), deps=PC_DONE + q_done + gtoks)
            t_mi = P.op("vector", lambda e: e.tensor_copy(out=rtp[:, :], in_=i32a[:, 0:128]), deps=[t_mi])
            t_mm1 = P.op("vector", lambda e, i=i: e.tensor_scalar(
                out=MnegS[:, 256 * i:256 * i + 128], in0=rtp[:, :],
                scalar1=sposS[:, i:i + 1], scalar2=NEG, op0=ALU.is_gt, op1=ALU.mult), deps=[t_mi])
            t_mi2 = P.op("gpsimd", lambda e, cap=cap: e.iota(i32a[:, 0:128], pattern=[[1, 128]], base=128 * (cap - 1), channel_multiplier=0), deps=[t_mi, t_mm1])
            t_mi2b = P.op("vector", lambda e: e.tensor_copy(out=rtp[:, :], in_=i32a[:, 0:128]), deps=[t_mi2])
            t_mm2 = P.op("vector", lambda e, i=i: e.tensor_scalar(
                out=MnegS[:, 256 * i + 128:256 * i + 256], in0=rtp[:, :],
                scalar1=sposS[:, i:i + 1], scalar2=NEG, op0=ALU.is_gt, op1=ALU.mult), deps=[t_mi2b])
            gtoks += [t_mm1, t_mm2]
        fq_toks = []
        for fq in range(4):
            mmt = []
            prev_ev = list(fq_toks)
            for ch in range(16):
                t_w = P.op("gpsimd", lambda e, ch=ch, fq=fq: e.dma_start(
                    out=wqb[ch % 2][:, :], in_=w["ffn_wq"][128 * ch:128 * ch + 128, 512 * fq:512 * fq + 512]),
                    deps=gtoks + mmt[-8:], dma=True)
                for sl in range(2):
                    mmt.append(P.op("tensor", lambda e, sl=sl, ch=ch: e.matmul(
                        psA[:, 512 * sl:512 * sl + 512], hoT[sl][:, 128 * ch:128 * ch + 128],
                        wqb[ch % 2][:, :], start=(ch == 0), stop=(ch == 15)),
                        deps=[t_w] + gtoks + (prev_ev if ch == 0 else [])))
            evs = []
            for sl in range(2):
                i = grp * 2 + sl
                evs.append(P.op("vector", lambda e, sl=sl, fq=fq, i=i: e.tensor_copy(
                    out=qrawS[sl][:, 512 * fq:512 * fq + 512], in_=psA[:, 512 * sl:512 * sl + 512]), deps=mmt))
                evs.append(P.op("scalar", lambda e, sl=sl, fq=fq: e.activation(
                    out=junk[:, 0:512], in_=psA[:, 512 * sl:512 * sl + 512], func=AF.Square,
                    accum_out=qsb[:, 4 * sl + fq:4 * sl + fq + 1]), deps=mmt + evs[-1:]))
            fq_toks = evs
        sl_toks = []
        for sl in range(2):
            i = grp * 2 + sl
            t = P.op("vector", lambda e, sl=sl: e.tensor_reduce(out=statb[:, 8:9], in_=qsb[:, 4 * sl:4 * sl + 4], axis=AX.X, op=ALU.add), deps=fq_toks + sl_toks)
            t = P.op("vector", lambda e: e.tensor_scalar(out=statb[:, 8:9], in0=statb[:, 8:9], scalar1=float(1.0 / D), scalar2=EPS_RMS, op0=ALU.mult, op1=ALU.add), deps=[t])
            t = P.op("scalar", lambda e: e.activation(out=statb[:, 8:9], in_=statb[:, 8:9], func=AF.Ln), deps=[t])
            t_rq = P.op("scalar", lambda e: e.activation(out=statb[:, 8:9], in_=statb[:, 8:9], func=AF.Exp, scale=-0.5), deps=[t])
            t = P.op("vector", lambda e, sl=sl: e.tensor_scalar(out=qrawS[sl][:, :], in0=qrawS[sl][:, :], scalar1=statb[:, 8:9], scalar2=None, op0=ALU.mult), deps=[t_rq])
            t_qn = P.op("vector", lambda e, sl=sl: e.tensor_tensor(out=qrawS[sl][:, :], in0=qrawS[sl][:, :], in1=QNW[:, :], op=ALU.mult), deps=[t])
            t_sq = P.op("scalar", lambda e, sl=sl: e.activation(out=SQb[:, :], in_=qrawS[sl][:, :], func=AF.Square), deps=[t_qn])
            t_hn = P.op("vector", lambda e: e.tensor_reduce(out=qhn[:, :], in_=SQb[:, :].rearrange("p (h d) -> p h d", h=16), axis=AX.X, op=ALU.add), deps=[t_sq])
            t_hn = P.op("scalar", lambda e: e.activation(out=qhn[:, :], in_=qhn[:, :], func=AF.Sqrt), deps=[t_hn])
            t_bd = P.op("vector", lambda e, i=i: e.tensor_scalar(out=bndS[:, 16 * i:16 * i + 16], in0=qhn[:, :], scalar1=KmxB[:, :], scalar2=None, op0=ALU.mult), deps=[t_hn, t_kmx])
            t_s, t_c = slot_tables(sposS[:, i:i + 1], [t_bd])
            rp = [t_c, t_s, t_qn]
            for hh in range(16):
                c0 = 128 * hh
                tA = P.op("vector", lambda e, c0=c0, sl=sl: e.tensor_tensor(out=rtp[:, 0:32], in0=qrawS[sl][:, c0:c0 + 32], in1=cqt[:, :], op=ALU.mult), deps=rp)
                tB = P.op("vector", lambda e, c0=c0, sl=sl: e.tensor_tensor(out=rtp[:, 32:64], in0=qrawS[sl][:, c0 + 32:c0 + 64], in1=sqt[:, :], op=ALU.mult), deps=rp)
                tC = P.op("vector", lambda e, c0=c0, sl=sl: e.tensor_tensor(out=rtp[:, 64:96], in0=qrawS[sl][:, c0 + 32:c0 + 64], in1=cqt[:, :], op=ALU.mult), deps=rp)
                tD = P.op("vector", lambda e, c0=c0, sl=sl: e.tensor_tensor(out=rtp[:, 96:128], in0=qrawS[sl][:, c0:c0 + 32], in1=sqt[:, :], op=ALU.mult), deps=rp)
                t1 = P.op("vector", lambda e, c0=c0, sl=sl: e.tensor_tensor(out=qrawS[sl][:, c0:c0 + 32], in0=rtp[:, 0:32], in1=rtp[:, 32:64], op=ALU.subtract), deps=[tA, tB, tC, tD])
                rp = [P.op("vector", lambda e, c0=c0, sl=sl: e.tensor_tensor(out=qrawS[sl][:, c0 + 32:c0 + 64], in0=rtp[:, 64:96], in1=rtp[:, 96:128], op=ALU.add), deps=[t1])]
                rp += [t_c]
            prev_e = {0: rp, 1: rp}
            for g4 in range(4):
                pt = g4 % 2
                trs = []
                for k4 in range(4):
                    hh = g4 * 4 + k4
                    trs.append(P.op("tensor", lambda e, pt=pt, k4=k4, hh=hh, sl=sl: e.transpose(
                        psT[pt][:, 128 * k4:128 * k4 + 128], qrawS[sl][:, 128 * hh:128 * hh + 128], ident[:, :]),
                        deps=rp + [t_id] + prev_e[pt]))
                tev = P.op("vector", lambda e, pt=pt, g4=g4, i=i: e.tensor_copy(
                    out=qT[:, :].rearrange("p (h i q) -> p h i q", h=16, i=16)[:, 4 * g4:4 * g4 + 4, i, :],
                    in_=psT[pt][:, 0:512].rearrange("p (k q) -> p k q", k=4)), deps=trs)
                prev_e[pt] = [tev]
                sl_toks.append(tev)
        q_done = sl_toks + fq_toks

    P2A_DONE = P.barrier()
    st2a.close()
    P.fence(P2A_DONE)


    prev_oe = []
    prev_oed = []
    prev_exp = {0: [], 1: []}
    for hh in range(16):
        kvh = hh // 4
        for i in range(16):
            cap = CAPS[i]
            kvlen = cap * 128
            qcol = (hh * 16 + i) * 128
            exp_toks = []
            n512 = (kvlen + 511) // 512
            for ck in range(n512):
                c0 = ck * 512
                cw = min(512, kvlen - c0)
                pg = ck % 2
                mm = P.op("tensor", lambda e, pg=pg, cw=cw, qcol=qcol, kvh=kvh, c0=c0: e.matmul(
                    psA[:, 512 * pg:512 * pg + cw], qT[:, qcol:qcol + 128],
                    KT[:, kvh * S + c0:kvh * S + c0 + cw], start=True, stop=True),
                    deps=prev_exp[pg] + prev_oe)
                mtoks = [mm]
                if c0 + cw > kvlen - 256:
                    s0 = max(c0, kvlen - 256)
                    moff = s0 - (kvlen - 256)
                    mlen = c0 + cw - s0
                    mtoks.append(P.op("vector", lambda e, pg=pg, s0=s0, c0=c0, moff=moff, mlen=mlen, i=i: e.tensor_tensor(
                        out=psA[:, 512 * pg + (s0 - c0):512 * pg + (s0 - c0) + mlen],
                        in0=psA[:, 512 * pg + (s0 - c0):512 * pg + (s0 - c0) + mlen],
                        in1=MnegS[:, 256 * i + moff:256 * i + moff + mlen], op=ALU.add), deps=[mm]))
                texp = P.op("scalar", lambda e, pg=pg, cw=cw, c0=c0, ck=ck, i=i, hh=hh: e.activation(
                    out=Astr[:, c0:c0 + cw], in_=psA[:, 512 * pg:512 * pg + cw], func=AF.Exp,
                    bias=bndS[:, 16 * i + hh:16 * i + hh + 1], scale=SCALE_F,
                    accum_out=dnmb[:, ck:ck + 1]), deps=mtoks)
                exp_toks.append(texp)
                prev_exp[pg] = [texp]
            t_d = P.op("vector", lambda e, n512=n512: e.tensor_reduce(out=rcpb[:, :], in_=dnmb[:, 0:n512], axis=AX.X, op=ALU.add), deps=exp_toks)
            t_d = P.op("vector", lambda e: e.reciprocal(out=rcpb[:, :], in_=rcpb[:, :]), deps=[t_d])
            t_rs = P.op("vector", lambda e, kvlen=kvlen: e.tensor_scalar(
                out=Astr[:, 0:kvlen], in0=Astr[:, 0:kvlen], scalar1=rcpb[:, :], scalar2=None, op0=ALU.mult), deps=[t_d])
            prev_at = {0: [t_rs], 1: [t_rs]}
            avmms = []
            for kt in range(cap):
                pt = kt % 2
                ttr = P.op("tensor", lambda e, pt=pt, kt=kt: e.transpose(
                    psT[pt][:, 0:128], Astr[:, kt * 128:kt * 128 + 128], ident[:, :]),
                    deps=[t_rs, t_id] + prev_at[pt])
                tev = P.op("vector", lambda e, pt=pt: e.tensor_copy(out=ATb[pt][:, :], in_=psT[pt][:, 0:128]), deps=[ttr])
                avmms.append(P.op("tensor", lambda e, pt=pt, kt=kt, kvh=kvh, cap=cap: e.matmul(
                    psC[:, 0:128], VS[:, kt * 512 + kvh * 128:kt * 512 + kvh * 128 + 128],
                    ATb[pt][:, :], start=(kt == 0), stop=(kt == cap - 1)),
                    deps=[tev] + (prev_oe if kt == 0 else [])))
                prev_at[pt] = [avmms[-1]]
            t_oe = P.op("vector", lambda e: e.tensor_copy(out=otb[:, :], in_=psC[:, 0:128]), deps=avmms + prev_oed)
            t_od = P.op("sync", lambda e, hh=hh, i=i: e.dma_start(
                out=otd[128 * hh:128 * hh + 128, 128 * i:128 * i + 128], in_=otb[:, :]), deps=[t_oe], dma=True)
            prev_oe = [t_oe]
            prev_oed = [t_od]

    P2B_DONE = P.barrier()
    stat2.close()
    stq.close()
    stA.close()
    P.fence(P2B_DONE)

    acc = sb("acc", [128, NOWN * D], BF16, stacc)
    cmbS = sb("cmbS", [128, NOWN * E], F32, stmoe)
    WOsb = sb("WOsb", [128, 16 * D], BF16, st2c)
    OTs = sb("OTs", [128, 16 * 128], BF16, st2c)
    h2T = sb("h2T", [128, D], BF16, st2c)
    sgb = sb("sgb", [128, E], F32, st2c)
    yb8 = sb("yb8", [128, E], F32, st2c)
    selb = sb("selb", [128, E], F32, st2c)
    tmp8 = sb("tmp8", [128, E], F32, st2c)
    iotam = sb("iotam", [128, E], F32, st2c)

    t_wo = []
    for ch in range(16):
        t_wo.append(P.op("gpsimd", lambda e, ch=ch: e.dma_start(
            out=WOsb[:, ch * D:(ch + 1) * D],
            in_=w["ffn_wo"][128 * ch:128 * ch + 128, :]), dma=True))
    t_im = P.op("vector", lambda e: e.tensor_scalar(out=iotam[:, :], in0=iota8[:, :], scalar1=-99.0, scalar2=None, op0=ALU.add))

    p2c_done = []
    for i in range(16):
        t_g = P.op("gpsimd", lambda e, i=i: e.indirect_dma_start(
            out=hob[:, :], out_offset=None, in_=x2d,
            in_offset=bass.IndirectOffsetOnAxis(idxS[:, i:i + 1], 0)), deps=p2c_done, dma=True)
        t_ot = P.op("sync", lambda e, i=i: e.dma_start(
            out=OTs[:, :].rearrange("p (c q) -> p c q", c=16),
            in_=otd.rearrange("(c p) q -> p c q", p=128)[:, :, 128 * i:128 * i + 128]), deps=p2c_done, dma=True)
        t_st = rowstats(hob[:, :], D, EPS_LN, [t_g])
        t_z = P.op("vector", lambda e: e.tensor_scalar(out=junk[:, :], in0=hob[:, :], scalar1=statb[:, 3:4], scalar2=statb[:, 4:5], op0=ALU.mult, op1=ALU.add), deps=[t_st])
        womms = []
        for ch in range(16):
            for dq in range(4):
                womms.append(P.op("tensor", lambda e, ch=ch, dq=dq: e.matmul(
                    psA[:, 512 * dq:512 * dq + 512], OTs[:, ch * 128:(ch + 1) * 128],
                    WOsb[:, ch * D + 512 * dq:ch * D + 512 * dq + 512],
                    start=(ch == 0), stop=(ch == 15)), deps=t_wo + p2c_done + [t_ot]))
        ev1 = []
        for dq in range(4):
            ev1.append(P.op("vector", lambda e, dq=dq: e.tensor_tensor(
                out=x1b[:, 512 * dq:512 * dq + 512], in0=psA[:, 512 * dq:512 * dq + 512],
                in1=junk[:, 512 * dq:512 * dq + 512], op=ALU.add), deps=womms + [t_z]))
        t_sx = P.op("sync", lambda e, i=i: e.dma_start(out=x1d[128 * i:128 * i + 128, :], in_=x1b[:, :]), deps=ev1, dma=True)
        t_rs = rmsstats(x1b[:, :], D, EPS_RMS, ev1)
        t_h2 = P.op("vector", lambda e: e.tensor_scalar(out=h2b[:, :], in0=x1b[:, :], scalar1=statb[:, 6:7], scalar2=None, op0=ALU.mult), deps=[t_rs])
        t_h2 = P.op("vector", lambda e: e.tensor_tensor(out=h2b[:, :], in0=h2b[:, :], in1=LN2W[:, :], op=ALU.mult), deps=[t_h2])
        prev_e = {0: [t_h2], 1: [t_h2]}
        h2evs = []
        for g4 in range(4):
            pt = g4 % 2
            trs = []
            for k4 in range(4):
                ch = g4 * 4 + k4
                trs.append(P.op("tensor", lambda e, pt=pt, k4=k4, ch=ch: e.transpose(
                    psT[pt][:, 128 * k4:128 * k4 + 128], h2b[:, 128 * ch:128 * ch + 128], ident[:, :]),
                    deps=[t_h2, t_id] + prev_e[pt]))
            tev = P.op("vector", lambda e, pt=pt, g4=g4: e.tensor_copy(out=h2T[:, 512 * g4:512 * g4 + 512], in_=psT[pt][:, 0:512]), deps=trs)
            prev_e[pt] = [tev]
            h2evs.append(tev)
        t_sh = P.op("sync", lambda e, i=i: e.dma_start(
            out=h2td.rearrange("(c p) q -> p c q", p=128)[:, :, 128 * i:128 * i + 128],
            in_=h2T[:, :].rearrange("p (c q) -> p c q", c=16)), deps=h2evs, dma=True)
        rmms = []
        for ch in range(16):
            rmms.append(P.op("tensor", lambda e, ch=ch: e.matmul(
                psC[:, 512:520], h2T[:, ch * 128:(ch + 1) * 128], RW[:, ch * E:(ch + 1) * E],
                start=(ch == 0), stop=(ch == 15)), deps=h2evs + p2c_done))
        t_sg = P.op("scalar", lambda e: e.activation(out=sgb[:, :], in_=psC[:, 512:520], func=AF.Sigmoid), deps=rmms)
        t_y8 = P.op("vector", lambda e: e.tensor_tensor(out=yb8[:, :], in0=sgb[:, :], in1=biasB[:, :], op=ALU.add), deps=[t_sg])
        tt = P.op("vector", lambda e: e.tensor_reduce(out=statb[:, 9:10], in_=yb8[:, :], axis=AX.X, op=ALU.max), deps=[t_y8])
        tt = P.op("vector", lambda e: e.tensor_scalar(out=selb[:, :], in0=yb8[:, :], scalar1=statb[:, 9:10], scalar2=None, op0=ALU.is_ge), deps=[tt])
        tt = P.op("vector", lambda e: e.tensor_tensor(out=tmp8[:, :], in0=selb[:, :], in1=iotam[:, :], op=ALU.mult), deps=[tt, t_im])
        tt = P.op("vector", lambda e: e.tensor_reduce(out=statb[:, 10:11], in_=tmp8[:, :], axis=AX.X, op=ALU.min), deps=[tt])
        tt = P.op("vector", lambda e: e.tensor_scalar(out=statb[:, 10:11], in0=statb[:, 10:11], scalar1=99.0, scalar2=None, op0=ALU.add), deps=[tt])
        tt = P.op("vector", lambda e: e.tensor_scalar(out=selb[:, :], in0=iota8[:, :], scalar1=statb[:, 10:11], scalar2=None, op0=ALU.is_equal), deps=[tt])
        tt = P.op("vector", lambda e: e.tensor_tensor(out=tmp8[:, :], in0=sgb[:, :], in1=selb[:, :], op=ALU.mult), deps=[tt])
        t_a1 = P.op("vector", lambda e: e.tensor_reduce(out=statb[:, 11:12], in_=tmp8[:, :], axis=AX.X, op=ALU.add), deps=[tt])
        tt = P.op("vector", lambda e: e.scalar_tensor_tensor(out=yb8[:, :], in0=selb[:, :], scalar=NEG, in1=yb8[:, :], op0=ALU.mult, op1=ALU.add), deps=[t_a1])
        t_s1k = P.op("vector", lambda e: e.tensor_copy(out=tmp8[:, :], in_=selb[:, :]), deps=[tt])
        tt = P.op("vector", lambda e: e.tensor_reduce(out=statb[:, 12:13], in_=yb8[:, :], axis=AX.X, op=ALU.max), deps=[t_s1k])
        tt = P.op("vector", lambda e: e.tensor_scalar(out=selb[:, :], in0=yb8[:, :], scalar1=statb[:, 12:13], scalar2=None, op0=ALU.is_ge), deps=[tt])
        tt = P.op("vector", lambda e: e.tensor_tensor(out=yb8[:, :], in0=selb[:, :], in1=iotam[:, :], op=ALU.mult), deps=[tt])
        tt = P.op("vector", lambda e: e.tensor_reduce(out=statb[:, 13:14], in_=yb8[:, :], axis=AX.X, op=ALU.min), deps=[tt])
        tt = P.op("vector", lambda e: e.tensor_scalar(out=statb[:, 13:14], in0=statb[:, 13:14], scalar1=99.0, scalar2=None, op0=ALU.add), deps=[tt])
        tt = P.op("vector", lambda e: e.tensor_scalar(out=selb[:, :], in0=iota8[:, :], scalar1=statb[:, 13:14], scalar2=None, op0=ALU.is_equal), deps=[tt])
        tt = P.op("vector", lambda e: e.tensor_tensor(out=yb8[:, :], in0=sgb[:, :], in1=selb[:, :], op=ALU.mult), deps=[tt])
        t_a2 = P.op("vector", lambda e: e.tensor_reduce(out=statb[:, 14:15], in_=yb8[:, :], axis=AX.X, op=ALU.add), deps=[tt])
        tt = P.op("vector", lambda e: e.tensor_tensor(out=statb[:, 15:16], in0=statb[:, 11:12], in1=statb[:, 14:15], op=ALU.add), deps=[t_a2])
        tt = P.op("vector", lambda e: e.reciprocal(out=statb[:, 15:16], in_=statb[:, 15:16]), deps=[tt])
        tt = P.op("vector", lambda e: e.tensor_tensor(out=statb[:, 11:12], in0=statb[:, 11:12], in1=statb[:, 15:16], op=ALU.mult), deps=[tt])
        tt = P.op("vector", lambda e: e.tensor_tensor(out=statb[:, 14:15], in0=statb[:, 14:15], in1=statb[:, 15:16], op=ALU.mult), deps=[tt])
        tt = P.op("vector", lambda e, i=i: e.tensor_scalar(out=cmbS[:, E * i:E * i + E], in0=tmp8[:, :], scalar1=statb[:, 11:12], scalar2=None, op0=ALU.mult), deps=[tt])
        tt = P.op("vector", lambda e, i=i: e.scalar_tensor_tensor(out=cmbS[:, E * i:E * i + E], in0=selb[:, :], scalar=statb[:, 14:15], in1=cmbS[:, E * i:E * i + E], op0=ALU.mult, op1=ALU.add), deps=[tt])
        p2c_done = [tt, t_sx, t_sh]

    P2C_DONE = P.barrier()
    st2c.close()
    P.fence(P2C_DONE)

    WGsb = sb("WGsb", [128, 16 * MI], BF16, stmoe)
    WUsb = sb("WUsb", [128, 16 * MI], BF16, stmoe)
    WDsb = sb("WDsb", [128, 4 * D], BF16, stmoe)
    h2ts = sb("h2ts", [128, 16 * 512], BF16, stmoe)
    act = sb("act", [128, 8 * 512], BF16, stmoe)
    sgl = sb("sgl", [128, 512], BF16, stmoe)
    upl = sb("upl", [128, 512], BF16, stmoe)
    dnl = sb("dnl", [128, 512], BF16, stmoe)

    t_z0 = P.op("vector", lambda e: e.memset(acc[:, :], 0.0))
    p3_prev = [t_z0]
    for ex in range(E):
        t_wg = P.op("gpsimd", lambda e, ex=ex: e.dma_start(
            out=WGsb[:, :].rearrange("p (c i) -> p c i", c=16),
            in_=w["ffn_wg"][ex].rearrange("(c p) i -> p c i", p=128)), deps=p3_prev, dma=True)
        t_wu = P.op("gpsimd", lambda e, ex=ex: e.dma_start(
            out=WUsb[:, :].rearrange("p (c i) -> p c i", c=16),
            in_=w["ffn_wu"][ex].rearrange("(c p) i -> p c i", p=128)), deps=p3_prev, dma=True)

        sc_prev = []
        for sc in range(4):
            t_h2 = P.op("sync", lambda e, sc=sc: e.dma_start(
                out=h2ts[:, :].rearrange("p (c q) -> p c q", c=16),
                in_=h2td.rearrange("(c p) q -> p c q", p=128)[:, :, 512 * sc:512 * sc + 512]),
                deps=p3_prev + sc_prev, dma=True)
            act_toks = []
            for it in range(8):
                gm = []
                um = []
                for ch in range(16):
                    gm.append(P.op("tensor", lambda e, ch=ch, it=it: e.matmul(
                        psA[:, 0:512], WGsb[:, ch * MI + it * 128:ch * MI + it * 128 + 128],
                        h2ts[:, ch * 512:(ch + 1) * 512], start=(ch == 0), stop=(ch == 15)),
                        deps=[t_wg, t_h2] + act_toks[-2:]))
                    um.append(P.op("tensor", lambda e, ch=ch, it=it: e.matmul(
                        psA[:, 512:1024], WUsb[:, ch * MI + it * 128:ch * MI + it * 128 + 128],
                        h2ts[:, ch * 512:(ch + 1) * 512], start=(ch == 0), stop=(ch == 15)),
                        deps=[t_wu, t_h2] + act_toks[-2:]))
                t_si = P.op("scalar", lambda e: e.activation(out=sgl[:, :], in_=psA[:, 0:512], func=AF.Silu), deps=gm)
                t_uc = P.op("vector", lambda e: e.tensor_copy(out=upl[:, :], in_=psA[:, 512:1024]), deps=um)
                t_ac = P.op("vector", lambda e, it=it: e.tensor_tensor(
                    out=act[:, 512 * it:512 * it + 512], in0=sgl[:, :], in1=upl[:, :], op=ALU.mult), deps=[t_si, t_uc])
                act_toks.append(t_ac)
                act_toks.append(t_si)
            dn_toks = []
            for ih in range(2):
                t_wdh = [P.op("gpsimd", lambda e, ex=ex, ih=ih: e.dma_start(
                    out=WDsb[:, :].rearrange("p (c f) -> p c f", c=4),
                    in_=w["ffn_wd"][ex, 512 * ih:512 * ih + 512].rearrange("(c p) f -> p c f", p=128)),
                    deps=p3_prev + dn_toks[-8:] + act_toks[-2:], dma=True)]
                for stl in range(4):
                    slot = sc * 4 + stl
                    for dq in range(4):
                        pg = dq % 2
                        dm = []
                        for ic4 in range(4):
                            ic = ih * 4 + ic4
                            dm.append(P.op("tensor", lambda e, ic=ic, ic4=ic4, stl=stl, dq=dq, pg=pg: e.matmul(
                                psC[:, 512 * pg:512 * pg + 512],
                                act[:, 512 * ic + 128 * stl:512 * ic + 128 * stl + 128],
                                WDsb[:, ic4 * D + 512 * dq:ic4 * D + 512 * dq + 512],
                                start=(ic4 == 0), stop=(ic4 == 3)),
                                deps=act_toks + dn_toks[-4:] + t_wdh))
                        t_cp = P.op("scalar", lambda e, pg=pg: e.copy(out=dnl[:, :], in_=psC[:, 512 * pg:512 * pg + 512]), deps=dm)
                        t_up = P.op("vector", lambda e, slot=slot, dq=dq, ex=ex: e.scalar_tensor_tensor(
                            out=acc[:, slot * D + 512 * dq:slot * D + 512 * dq + 512],
                            in0=dnl[:, :], scalar=cmbS[:, E * slot + ex:E * slot + ex + 1],
                            in1=acc[:, slot * D + 512 * dq:slot * D + 512 * dq + 512],
                            op0=ALU.mult, op1=ALU.add), deps=[t_cp, t_z0])
                        dn_toks += [t_cp, t_up]
            sc_prev = dn_toks[-8:] + act_toks[-2:]
        p3_prev = sc_prev

    P3_DONE = P.barrier()
    stmoe.close()
    P.fence(P3_DONE)

    ob = sb("ob", [128, D], F32, stp4)
    p4_prev = []
    for i in range(16):
        t_x1 = P.op("sync", lambda e, i=i: e.dma_start(out=x1b[:, :], in_=x1d[128 * i:128 * i + 128, :]), deps=p4_prev, dma=True)
        t_g = P.op("gpsimd", lambda e, i=i: e.indirect_dma_start(
            out=hob[:, :], out_offset=None, in_=x2d,
            in_offset=bass.IndirectOffsetOnAxis(idxS[:, i:i + 1], 0)), deps=p4_prev, dma=True)
        t_f = P.op("vector", lambda e, i=i: e.tensor_tensor(
            out=h2b[:, :], in0=x1b[:, :], in1=acc[:, i * D:(i + 1) * D], op=ALU.add), deps=[t_x1])
        t_st = rowstats(h2b[:, :], D, EPS_LN, [t_f])
        t_n = P.op("vector", lambda e: e.tensor_scalar(out=junk[:, :], in0=h2b[:, :], scalar1=statb[:, 3:4], scalar2=statb[:, 4:5], op0=ALU.mult, op1=ALU.add), deps=[t_st])
        t_o = P.op("vector", lambda e: e.tensor_tensor(out=ob[:, :], in0=junk[:, :], in1=hob[:, :], op=ALU.add), deps=[t_n, t_g])
        t_w = P.op("sync", lambda e, i=i: e.dma_start(out=outd[128 * i:128 * i + 128, :], in_=ob[:, :]), deps=[t_o], dma=True)
        p4_prev = [t_w, t_o, t_f]

    stp4.close()
    stacc.close()
    with nc.allow_low_precision(reason="bf16 activation stream; reductions run in f32 internally"):
        P.emit(nc, sems)
    nc._keep = (st,)
    return nc


_CACHED = {}


def kernel(**inputs):
    if "nc" not in _CACHED:
        _CACHED["nc"] = build_program()
    nc = _CACHED["nc"]
    x = np.asarray(inputs["L0_attn_x"])
    in_maps = []
    for core in range(8):
        b, j = core // 2, core % 2
        tiles = own_tiles(j)
        idx = np.zeros((128, NOWN), np.uint32)
        pos = np.zeros((128, NOWN), np.float32)
        for i, T in enumerate(tiles):
            rows = np.arange(128 * T, 128 * T + 128)
            idx[:, i] = rows
            pos[:, i] = rows
        m = {"x": np.ascontiguousarray(x[b]), "own_idx": idx, "own_pos": pos}
        for k in ("attn_qkv_w", "attn_conv_b0",
                  "attn_conv_b1", "attn_proj_w", "attn_proj_b", "attn_ln_w",
                  "attn_ln_b", "ffn_ln1_w", "ffn_ln2_w", "ffn_wq", "ffn_wk",
                  "ffn_wv", "ffn_wo", "ffn_qnorm_w", "ffn_knorm_w",
                  "ffn_router_w", "ffn_router_bias", "ffn_wg", "ffn_wu", "ffn_wd"):
            m[k] = np.ascontiguousarray(np.asarray(inputs["L0_" + k]), dtype=np.float32)
        cv = np.zeros(256, np.float32)
        cv[0:72] = np.asarray(inputs["L0_attn_conv_w0"], np.float32).ravel()
        cv[128:200] = np.asarray(inputs["L0_attn_conv_w1"], np.float32).ravel()
        m["conv_all"] = cv
        in_maps.append(m)
    res = run_bass_kernel_spmd(nc, in_maps, core_ids=list(range(8)))
    out = np.zeros((B, S, D), np.float32)
    for core in range(8):
        b, j = core // 2, core % 2
        r = res.results[core]["out"]
        for i, T in enumerate(own_tiles(j)):
            out[b, 128 * T:128 * T + 128, :] = r[128 * i:128 * i + 128, :]
    return out


# version 10
# speedup vs baseline: 1.0075x; 1.0075x over previous
import math
import contextlib
import numpy as np
import ml_dtypes
import concourse.bass as bass
import concourse.mybir as mybir
from concourse.bass_utils import run_bass_kernel_spmd

F32 = mybir.dt.float32
BF16 = mybir.dt.bfloat16
I32 = mybir.dt.int32
U32 = mybir.dt.uint32
AF = mybir.ActivationFunctionType
ALU = mybir.AluOpType
AX = mybir.AxisListType

B, S, D = 4, 4096, 2048
NT, NOWN = 32, 16
H, KV, DH, RD = 16, 4, 128, 64
E, MI = 8, 1024
THETA = 1e6
EPS_LN = 1e-5
EPS_RMS = 1e-6
NEG = -1e30
SCALE_B = float(1.0 / math.sqrt(8.0))
SCALE_F = float(1.0 / math.sqrt(128.0))

ENGS = ("sync", "vector", "scalar", "tensor", "gpsimd")


def own_tiles(j):
    out = []
    for u in range(8):
        out += ([4 * u, 4 * u + 3] if j == 0 else [4 * u + 1, 4 * u + 2])
    return out


CAPS = []
for _u in range(8):
    CAPS += [4 * _u + 2, 4 * _u + 4]

TWO_PI_HI = float(np.float32(2 * np.pi))
_r1 = 2 * np.pi - np.float64(TWO_PI_HI)
TWO_PI_MID = float(np.float32(_r1))
TWO_PI_LO = float(_r1 - np.float64(TWO_PI_MID))


import os
DEDUP = os.environ.get("KDEDUP", "1") == "1"
KTRUNC = int(os.environ.get("KTRUNC", "0"))
KONEQ = os.environ.get("KONEQ", "0") == "1"
KSKIP = int(os.environ.get("KSKIP", "0"))
HAZN = int(os.environ.get("KHAZN", "4"))


class Plan:
    def __init__(self):
        self.ops = []
        self.cnt = {}
        for e in ENGS:
            self.cnt["C" + e] = 0
            self.cnt["D" + e] = 0
        self.hiwater = {e: {} for e in ENGS}

    def fence(self, toks):
        self.pending = list(toks)

    def op(self, eng, fn, deps=(), dma=False):
        if dma and KONEQ and eng == "sync":
            eng = "gpsimd"
        sem = ("D" if dma else "C") + eng
        inc = 16 if dma else 1
        waits = []
        hw = self.hiwater[eng]
        for (s_, v) in list(deps) + getattr(self, "pending", []):
            if s_ == "C" + eng and not dma:
                if v <= self.cnt[s_] - HAZN:
                    continue
            if dma and s_ == "D" + eng:
                continue
            if DEDUP and hw.get(s_, -1) >= v:
                continue
            hw[s_] = v
            waits.append((s_, v))
        self.cnt[sem] += inc
        self.ops.append((eng, fn, sem, inc, waits))
        return (sem, self.cnt[sem])

    def barrier(self):
        return [(s_, v) for s_, v in self.cnt.items() if v > 0]

    def emit(self, nc, sems):
        by_eng = {e: [] for e in ENGS}
        for eng, fn, sem, inc, waits in self.ops:
            by_eng[eng].append((fn, sem, inc, waits))
        final = self.barrier()
        with nc.Block() as block:
            for e in ENGS:
                def make(ee):
                    def body(engine):
                        for fn, sem, inc, waits in by_eng[ee]:
                            if not waits and inc == 16:
                                engine.wait_ge(sems["C" + ee], 0)
                            for (s_, v) in waits:
                                engine.wait_ge(sems[s_], v)
                            fn(engine).then_inc(sems[sem], inc)
                        if ee == "sync":
                            for (s_, v) in final:
                                engine.wait_ge(sems[s_], v)
                    return body
                getattr(block, e)(make(e))


def build_program(dev=False):
    nc = bass.Bass("TRN2", target_bir_lowering=False)
    SCR = "ExternalOutput" if dev else "Internal"

    xd = nc.dram_tensor("x", [S, D], F32, kind="ExternalInput").ap()
    idxd = nc.dram_tensor("own_idx", [128, NOWN], U32, kind="ExternalInput").ap()
    sposd = nc.dram_tensor("own_pos", [128, NOWN], F32, kind="ExternalInput").ap()
    wspec = {
        "attn_qkv_w": [16, 48], "conv_all": [256], "attn_conv_b0": [8],
        "attn_conv_b1": [8], "attn_proj_w": [16, 16],
        "attn_proj_b": [16], "attn_ln_w": [16], "attn_ln_b": [16],
        "ffn_ln1_w": [D], "ffn_ln2_w": [D], "ffn_wq": [D, D], "ffn_wk": [D, 512],
        "ffn_wv": [D, 512], "ffn_wo": [D, D], "ffn_qnorm_w": [D],
        "ffn_knorm_w": [512], "ffn_router_w": [D, E], "ffn_router_bias": [E],
        "ffn_wg": [E, D, MI], "ffn_wu": [E, D, MI], "ffn_wd": [E, MI, D],
    }
    w = {k: nc.dram_tensor(k, shp, F32, kind="ExternalInput").ap() for k, shp in wspec.items()}
    outd = nc.dram_tensor("out", [NOWN * 128, D], F32, kind="ExternalOutput").ap()

    attd = nc.dram_tensor("att_scr", [S, D], BF16, kind=SCR).ap()
    x2d = nc.dram_tensor("x2_scr", [S, D], BF16, kind=SCR).ap()
    x1d = nc.dram_tensor("x1_scr", [NOWN * 128, D], BF16, kind=SCR).ap()
    h2td = nc.dram_tensor("h2t_scr", [16 * 128, NOWN * 128], BF16, kind=SCR).ap()
    otd = nc.dram_tensor("ot_scr", [16 * 128, NOWN * 128], BF16, kind=SCR).ap()
    dbgd = nc.dram_tensor("dbg_scr", [12, 128, D], BF16, kind=SCR).ap() if dev else None
    dbgf = nc.dram_tensor("dbgf_scr", [4, 128, 16], F32, kind=SCR).ap() if dev else None

    P = Plan()
    st = contextlib.ExitStack()
    sems = {s_: st.enter_context(nc.semaphore(s_)) for s_ in P.cnt}

    st1 = contextlib.ExitStack()
    stA = contextlib.ExitStack()
    stpc = contextlib.ExitStack()
    stt = contextlib.ExitStack()
    stq = contextlib.ExitStack()
    stat2 = contextlib.ExitStack()
    st2a = contextlib.ExitStack()
    stacc = contextlib.ExitStack()
    stmoe = contextlib.ExitStack()
    st2c = contextlib.ExitStack()
    stp4 = contextlib.ExitStack()

    def sb(name, shape, dt, stk=None):
        return (stk or st).enter_context(nc.sbuf_tensor(name, shape, dt))

    def ps(name, shape, dt):
        return st.enter_context(nc.psum_tensor(name, shape, dt))

    psA = ps("psA", [128, 2048], F32)
    psC = ps("psC", [128, 1024], F32)
    psX = ps("psX", [128, 1024], BF16)
    psY = ps("psY", [128, 1024], BF16)
    psT = [psX, psY]

    ident = sb("ident", [128, 128], BF16)
    i32a = sb("i32a", [128, 128], I32)
    onesc = sb("onesc", [128, 1], BF16)
    ones1 = sb("ones1", [1, 128], F32)
    rowbuf = sb("rowbuf", [1, 512], F32)
    cvst = sb("cvst", [1, 256], F32)
    statb = sb("statb", [128, 16], F32)
    junk = sb("junk", [128, D], BF16)
    SQb = sb("SQb", [128, D], BF16)
    hob = SQb
    x1b = sb("x1b", [128, D], BF16)
    h2b = sb("h2b", [128, D], BF16)

    t_c2 = P.op("sync", lambda e: e.dma_start(out=cvst[0:1, :], in_=w["conv_all"][None, :]), dma=True)
    t = P.op("gpsimd", lambda e: e.iota(i32a[:, 0:128], pattern=[[1, 128]], base=0, channel_multiplier=-1))
    t_id = P.op("vector", lambda e: e.tensor_scalar(out=ident[:, :], in0=i32a[:, 0:128], scalar1=0.0, scalar2=None, op0=ALU.is_equal), deps=[t])
    t_onesc = P.op("vector", lambda e: e.memset(onesc[:, :], 1.0))
    t_ones1 = P.op("vector", lambda e: e.memset(ones1[:, :], 1.0))

    last_b = [t_ones1]

    def bcast_row(dst_ap, src_row_ap, n):
        toks = []
        for b0 in range(0, n, 512):
            bw = min(512, n - b0)
            t_l = P.op("sync", lambda e, b0=b0, bw=bw: e.dma_start(
                out=rowbuf[0:1, 0:bw], in_=src_row_ap[:, b0:b0 + bw]), deps=last_b + toks, dma=True)
            for c0 in range(0, bw, 512):
                cw = min(512, bw - c0)
                t_m = P.op("tensor", lambda e, c0=c0, cw=cw: e.matmul(
                    psA[:, 0:cw], ones1[0:1, :], rowbuf[0:1, c0:c0 + cw], start=True, stop=True),
                    deps=[t_l, t_ones1] + toks)
                toks.append(P.op("vector", lambda e, b0=b0, c0=c0, cw=cw: e.tensor_copy(
                    out=dst_ap[:, b0 + c0:b0 + c0 + cw], in_=psA[:, 0:cw]), deps=[t_m]))
            last_b[:] = toks[-1:]
        return toks[-1]

    LN1W = sb("LN1W", [128, D], BF16)
    LN2W = sb("LN2W", [128, D], BF16)
    QNW = sb("QNW", [128, D], BF16)
    biasB = sb("biasB", [128, E], F32)
    iota8 = sb("iota8", [128, E], F32)
    KNWB = sb("KNWB", [128, 512], BF16)
    RW = sb("RW", [128, 16 * E], BF16)
    idxS = sb("idxS", [128, NOWN], U32)
    sposS = sb("sposS", [128, NOWN], F32)
    posb = sb("posb", [128, 1], F32)
    rtp = sb("rtp", [128, 128], F32)
    invfB = sb("invfB", [128, 32], F32)
    rsc = sb("rsc", [128, 32], F32)
    sqt = sb("sqt", [128, 32], F32)
    cqt = sb("cqt", [128, 32], F32)
    KT = sb("KT", [128, KV * S], BF16, stA)
    VS = sb("VS", [128, NT * 512], BF16, stA)
    xb0 = sb("xb0", [128, D], F32, stA)
    xb = [xb0, xb0]
    lnwB = sb("lnwB", [128, 16], F32, st1)
    lnbB = sb("lnbB", [128, 16], F32, st1)
    pbB = sb("pbB", [128, 16], F32, st1)
    cb1B = sb("cb1B", [128, 8], F32, st1)
    cb2B = sb("cb2B", [128, 8], F32, st1)
    cw1B = sb("cw1B", [128, 24], F32, st1)
    cw2B = sb("cw2B", [128, 24], F32, st1)

    if KSKIP >= 3:
        def bcast_row(*a, **k):
            return last_b[0]
    bcast_row(LN1W, w["ffn_ln1_w"][None, :], D)
    bcast_row(LN2W, w["ffn_ln2_w"][None, :], D)
    bcast_row(QNW, w["ffn_qnorm_w"][None, :], D)
    bcast_row(KNWB, w["ffn_knorm_w"][None, :], 512)
    bcast_row(lnwB, w["attn_ln_w"][None, :], 16)
    bcast_row(lnbB, w["attn_ln_b"][None, :], 16)
    bcast_row(pbB, w["attn_proj_b"][None, :], 16)
    bcast_row(biasB, w["ffn_router_bias"][None, :], E)
    bcast_row(cb1B, w["attn_conv_b0"][None, :], 8)
    bcast_row(cb2B, w["attn_conv_b1"][None, :], 8)
    tap_toks = []
    for dtap in range(3):
        for dst, seg, off in ((cw1B, 0, 3 + dtap), (cw2B, 128, 1 + 3 * dtap)):
            rhs = cvst[0:1, seg:seg + 72].rearrange("o (c n) -> o c n", n=9)[:, :, off]
            t_m = P.op("tensor", lambda e, rhs=rhs: e.matmul(
                psA[:, 0:8], ones1[0:1, :], rhs, start=True, stop=True),
                deps=[t_c2, t_ones1] + tap_toks)
            tap_toks.append(P.op("vector", lambda e, dst=dst, dtap=dtap: e.tensor_copy(
                out=dst[:, dtap * 8:dtap * 8 + 8], in_=psA[:, 0:8]), deps=[t_m]))
    last_b[:] = tap_toks[-1:]

    t = P.op("gpsimd", lambda e: e.iota(i32a[:, 0:8], pattern=[[1, 8]], base=0, channel_multiplier=0), deps=[t_id])
    P.op("vector", lambda e: e.tensor_copy(out=iota8[:, :], in_=i32a[:, 0:8]), deps=[t])

    Wq = sb("Wq", [128, 128], BF16, st1)
    Wk = sb("Wk", [128, 128], BF16, st1)
    Wv = sb("Wv", [128, 128], BF16, st1)
    Pbd = sb("Pbd", [128, 128], BF16, st1)
    t_z = [P.op("vector", lambda e, b_=b_: e.memset(b_[:, :], 0.0)) for b_ in (Wq, Wk, Wv, Pbd)]
    for gh in range(0 if KSKIP >= 1 else 8):
        for buf, sc in ((Wq, 0), (Wk, 16), (Wv, 32)):
            P.op("gpsimd", lambda e, buf=buf, gh=gh, sc=sc: e.dma_start(
                out=buf[16 * gh:16 * gh + 16, 16 * gh:16 * gh + 16],
                in_=w["attn_qkv_w"][:, sc:sc + 16]), deps=t_z, dma=True)
        P.op("gpsimd", lambda e, gh=gh: e.dma_start(
            out=Pbd[16 * gh:16 * gh + 16, 16 * gh:16 * gh + 16],
            in_=w["attn_proj_w"][:, :]), deps=t_z, dma=True)

    if KSKIP < 2:
        P.op("gpsimd", lambda e: e.dma_start(out=RW[:, :].rearrange("p (c f) -> p c f", c=16), in_=w["ffn_router_w"].rearrange("(c p) f -> p c f", p=128)), dma=True)

    P.op("sync", lambda e: e.dma_start(out=idxS[:, :], in_=idxd), dma=True)
    P.op("sync", lambda e: e.dma_start(out=sposS[:, :], in_=sposd), dma=True)

    CONST = P.barrier()
    P.fence(CONST)
    if os.environ.get("KMINI", "0") == "1":
        xbm = xb[0]
        snm = x1b
        dbgm = dbgd
        t_x = P.op("sync", lambda e: e.dma_start(out=xbm[:, :], in_=xd[0:128, :]), dma=True)
        t_s = P.op("vector", lambda e: e.tensor_copy(out=snm[:, :], in_=xbm[:, :]), deps=[t_x])
        P.op("sync", lambda e: e.dma_start(out=dbgm[5], in_=snm[:, :]), deps=[t_s], dma=True)
        P.emit(nc, sems)
        nc._keep = (st,)
        return nc



    t = P.op("gpsimd", lambda e: e.iota(i32a[:, 0:32], pattern=[[1, 32]], base=0, channel_multiplier=0), deps=[t_id])
    t = P.op("vector", lambda e: e.tensor_copy(out=invfB[:, :], in_=i32a[:, 0:32]), deps=[t])
    t_invB = P.op("scalar", lambda e: e.activation(out=invfB[:, :], in_=invfB[:, :], func=AF.Exp, scale=-float(np.log(THETA) / 32.0)), deps=[t])

    def rope_sin(dst, deps):
        t = P.op("vector", lambda e: e.tensor_scalar(out=rsc[:, :], in0=dst[:, :], scalar1=float(1.0 / (2 * np.pi)), scalar2=None, op0=ALU.mult), deps=deps)
        t = P.op("vector", lambda e: e.tensor_copy(out=i32a[:, 0:32], in_=rsc[:, :]), deps=[t])
        t = P.op("vector", lambda e: e.tensor_copy(out=rsc[:, :], in_=i32a[:, 0:32]), deps=[t])
        for cc in (TWO_PI_HI, TWO_PI_MID, TWO_PI_LO):
            t = P.op("vector", lambda e, cc=cc: e.scalar_tensor_tensor(out=dst[:, :], in0=rsc[:, :], scalar=-cc, in1=dst[:, :], op0=ALU.mult, op1=ALU.add), deps=[t])
        return P.op("scalar", lambda e: e.activation(out=dst[:, :], in_=dst[:, :], func=AF.Sin), deps=[t])

    def slot_tables(pos_scalar_ap, deps):
        t_a1 = P.op("vector", lambda e: e.tensor_scalar(out=sqt[:, :], in0=invfB[:, :], scalar1=pos_scalar_ap, scalar2=None, op0=ALU.mult), deps=deps + [t_invB])
        t_s = rope_sin(sqt, [t_a1])
        t_a2 = P.op("vector", lambda e: e.tensor_scalar(out=cqt[:, :], in0=invfB[:, :], scalar1=pos_scalar_ap, scalar2=float(np.pi / 2), op0=ALU.mult, op1=ALU.add), deps=[t_s])
        t_c = rope_sin(cqt, [t_a2])
        return t_s, t_c


    def rowstats(src_ap, n, eps_val, deps):
        t_s = P.op("vector", lambda e: e.tensor_reduce(out=statb[:, 0:1], in_=src_ap, axis=AX.X, op=ALU.add), deps=deps)
        if dev:
            t_s = P.op("vector", lambda e: e.tensor_copy(out=statb[:, 5:6], in_=statb[:, 0:1]), deps=[t_s])
        t_q = P.op("scalar", lambda e: e.activation(out=junk[:, 0:n], in_=src_ap, func=AF.Square, accum_out=statb[:, 1:2]), deps=deps + [t_s])
        t = P.op("vector", lambda e: e.tensor_scalar(out=statb[:, 0:1], in0=statb[:, 0:1], scalar1=float(1.0 / n), scalar2=None, op0=ALU.mult), deps=[t_s])
        t = P.op("vector", lambda e: e.tensor_scalar(out=statb[:, 1:2], in0=statb[:, 1:2], scalar1=float(1.0 / n), scalar2=None, op0=ALU.mult), deps=[t_q, t])
        t = P.op("vector", lambda e: e.tensor_tensor(out=statb[:, 2:3], in0=statb[:, 0:1], in1=statb[:, 0:1], op=ALU.mult), deps=[t])
        t = P.op("vector", lambda e: e.tensor_tensor(out=statb[:, 1:2], in0=statb[:, 1:2], in1=statb[:, 2:3], op=ALU.subtract), deps=[t])
        t = P.op("vector", lambda e, ev=eps_val: e.tensor_scalar(out=statb[:, 1:2], in0=statb[:, 1:2], scalar1=ev, scalar2=None, op0=ALU.add), deps=[t])
        t = P.op("scalar", lambda e: e.activation(out=statb[:, 1:2], in_=statb[:, 1:2], func=AF.Ln), deps=[t])
        t = P.op("scalar", lambda e: e.activation(out=statb[:, 3:4], in_=statb[:, 1:2], func=AF.Exp, scale=-0.5), deps=[t])
        t = P.op("vector", lambda e: e.tensor_tensor(out=statb[:, 4:5], in0=statb[:, 0:1], in1=statb[:, 3:4], op=ALU.mult), deps=[t])
        t = P.op("vector", lambda e: e.tensor_scalar(out=statb[:, 4:5], in0=statb[:, 4:5], scalar1=-1.0, scalar2=None, op0=ALU.mult), deps=[t])
        return t

    def rmsstats(src_ap, n, eps_val, deps, col=6):
        t_q = P.op("scalar", lambda e: e.activation(out=SQb[:, 0:n], in_=src_ap, func=AF.Square, accum_out=statb[:, col:col + 1]), deps=deps)
        t = P.op("vector", lambda e: e.tensor_scalar(out=statb[:, col:col + 1], in0=statb[:, col:col + 1], scalar1=float(1.0 / n), scalar2=eps_val, op0=ALU.mult, op1=ALU.add), deps=[t_q])
        t = P.op("scalar", lambda e: e.activation(out=statb[:, col:col + 1], in_=statb[:, col:col + 1], func=AF.Ln), deps=[t])
        t = P.op("scalar", lambda e: e.activation(out=statb[:, col:col + 1], in_=statb[:, col:col + 1], func=AF.Exp, scale=-0.5), deps=[t])
        return t

    yb = sb("yb", [128, D], BF16, st1)
    yT = sb("yT", [128, D], BF16, st1)
    Tq = sb("Tq", [128, D], BF16, st1)
    Tk = sb("Tk", [128, D], BF16, st1)
    Tv = sb("Tv", [128, D], BF16, st1)
    q2t = sb("q2t", [128, 1024], BF16, st1)
    k2t = sb("k2t", [128, 1024], BF16, st1)
    v2t = sb("v2t", [128, 1024], BF16, st1)
    OC = sb("OC", [128, D], BF16, st1)
    OC2 = sb("OC2", [128, 1024], BF16, st1)
    OCP = sb("OCP", [128, D], BF16, st1)
    NF = sb("NF", [128, D], BF16, st1)
    OTg = sb("OTg", [128, 128], BF16, st1)
    pr1 = sb("pr1", [128, 4096], BF16, st1)
    lv1 = sb("lv1", [128, 2048], BF16, st1)
    lv2 = sb("lv2", [128, 1024], BF16, st1)
    Sbuf = sb("Sbuf", [128, 512], F32, st1)
    mbuf = sb("mbuf", [128, 64], F32, st1)
    Ebuf = sb("Ebuf", [128, 512], BF16, st1)
    dbuf = sb("dbuf", [128, 64], F32, st1)
    rbuf = sb("rbuf", [128, 64], F32, st1)
    Enb = sb("Enb", [128, 512], BF16, st1)
    pr2 = sb("pr2", [128, 8 * 8 * 12], BF16, st1)
    mug = sb("mug", [128, 128], F32, st1)
    msg = sb("msg", [128, 128], F32, st1)
    rsg = sb("rsg", [128, 128], F32, st1)
    m2g = sb("m2g", [128, 128], F32, st1)

    def branch(nW, qv, kv_, vv, cwB, cbB, dst_fn, deps):
        toks = list(deps)
        aw = nW * 8
        tprod = []
        for a in range(8):
            tprod.append(P.op("vector", lambda e, a=a: e.tensor_tensor(
                out=pr1[:, a * nW * 64:(a + 1) * nW * 64].rearrange("p (w b c) -> p w b c", b=8, c=8),
                in0=qv(a), in1=kv_(a), op=ALU.mult), deps=toks))
        t = P.op("vector", lambda e: e.tensor_tensor(
            out=lv1[:, 0:aw * 32].rearrange("p (x c) -> p x c", c=4),
            in0=pr1[:, 0:aw * 64].rearrange("p (x c) -> p x c", c=8)[:, :, 0:4],
            in1=pr1[:, 0:aw * 64].rearrange("p (x c) -> p x c", c=8)[:, :, 4:8],
            op=ALU.add), deps=tprod)
        t = P.op("vector", lambda e: e.tensor_tensor(
            out=lv2[:, 0:aw * 16].rearrange("p (x c) -> p x c", c=2),
            in0=lv1[:, 0:aw * 32].rearrange("p (x c) -> p x c", c=4)[:, :, 0:2],
            in1=lv1[:, 0:aw * 32].rearrange("p (x c) -> p x c", c=4)[:, :, 2:4],
            op=ALU.add), deps=[t])
        t = P.op("vector", lambda e: e.tensor_tensor(
            out=Sbuf[:, 0:aw * 8],
            in0=lv2[:, 0:aw * 16].rearrange("p (x c) -> p x c", c=2)[:, :, 0:1],
            in1=lv2[:, 0:aw * 16].rearrange("p (x c) -> p x c", c=2)[:, :, 1:2],
            op=ALU.add), deps=[t])
        t = P.op("vector", lambda e: e.tensor_reduce(
            out=mbuf[:, 0:aw], in_=Sbuf[:, 0:aw * 8].rearrange("p (x b) -> p x b", b=8),
            axis=AX.X, op=ALU.max), deps=[t])
        t = P.op("vector", lambda e: e.tensor_tensor(
            out=Sbuf[:, 0:aw * 8].rearrange("p (x b) -> p x b", b=8),
            in0=Sbuf[:, 0:aw * 8].rearrange("p (x b) -> p x b", b=8),
            in1=mbuf[:, 0:aw][:, :, None].broadcast_to([128, aw, 8]),
            op=ALU.subtract), deps=[t])
        t = P.op("scalar", lambda e: e.activation(out=Ebuf[:, 0:aw * 8], in_=Sbuf[:, 0:aw * 8], func=AF.Exp, scale=SCALE_B), deps=[t])
        t = P.op("vector", lambda e: e.tensor_reduce(
            out=dbuf[:, 0:aw], in_=Ebuf[:, 0:aw * 8].rearrange("p (x b) -> p x b", b=8),
            axis=AX.X, op=ALU.add), deps=[t])
        t = P.op("vector", lambda e: e.reciprocal(out=rbuf[:, 0:aw], in_=dbuf[:, 0:aw]), deps=[t])
        t_en = P.op("vector", lambda e: e.tensor_tensor(
            out=Enb[:, 0:aw * 8].rearrange("p (x b) -> p x b", b=8),
            in0=Ebuf[:, 0:aw * 8].rearrange("p (x b) -> p x b", b=8),
            in1=rbuf[:, 0:aw][:, :, None].broadcast_to([128, aw, 8]),
            op=ALU.mult), deps=[t])
        outs = []
        for a in range(8):
            t2 = P.op("vector", lambda e, a=a: e.tensor_tensor(
                out=pr2[:, 0:nW * 96].rearrange("p (w c b) -> p w c b", c=8, b=12)[:, :, :, 0:8],
                in0=Enb[:, a * nW * 8:(a + 1) * nW * 8].rearrange("p (w b) -> p w b", b=8)[:, :, None, :].broadcast_to([128, nW, 8, 8]),
                in1=vv(a), op=ALU.mult), deps=[t_en] + outs[-1:])
            tl = [t2]
            for dt in range(3):
                av = a + dt - 1
                if av < 0 or av > 7:
                    tl.append(P.op("vector", lambda e, dt=dt: e.memset(
                        pr2[:, 0:nW * 96].rearrange("p (w c b) -> p w c b", c=8, b=12)[:, :, :, 8 + dt:9 + dt], 0.0), deps=[t2]))
                else:
                    tl.append(P.op("vector", lambda e, dt=dt, av=av: e.tensor_tensor(
                        out=pr2[:, 0:nW * 96].rearrange("p (w c b) -> p w c b", c=8, b=12)[:, :, :, 8 + dt:9 + dt],
                        in0=vv(av, True),
                        in1=cwB[:, dt * 8:dt * 8 + 8][:, None, :, None].broadcast_to([128, nW, 8, 1]),
                        op=ALU.mult), deps=[t2]))
            outs.append(P.op("vector", lambda e, a=a: e.tensor_reduce(
                out=dst_fn(a), in_=pr2[:, 0:nW * 96].rearrange("p (w c b) -> p w c b", c=8, b=12),
                axis=AX.X, op=ALU.add), deps=tl))
        return outs

    def fill_cb(cbB, nW, deps):
        return P.op("vector", lambda e: e.tensor_scalar(
            out=pr2[:, 0:nW * 96].rearrange("p (w c b) -> p w c b", c=8, b=12)[:, :, :, 11:12],
            in0=cbB[:, None, :, None].broadcast_to([128, nW, 8, 1]),
            scalar1=0.0, scalar2=None, op0=ALU.add), deps=deps)

    scatter_toks = {0: [], 1: []}
    prev_sb_done = CONST

    for ub in range(KTRUNC if KTRUNC else NT):
        r0 = 128 * ub
        pb = ub % 2
        t_x = P.op("sync", lambda e, r0=r0, pb=pb: e.dma_start(out=xb[pb][:, :], in_=xd[r0:r0 + 128, :]), deps=prev_sb_done, dma=True)
        if dev:
            for _pad in range(int(os.environ.get("KPAD", "1"))):
                t_x = P.op("sync", lambda e, r0=r0, pb=pb: e.dma_start(out=xb[pb][:, :], in_=xd[r0:r0 + 128, :]), deps=[t_x], dma=True)
        if dev and ub == 0:
            t_snap = P.op("vector", lambda e, pb=pb: e.tensor_copy(out=x1b[:, :], in_=xb[pb][:, :]), deps=[t_x])
            P.op("sync", lambda e: e.dma_start(out=dbgd[8], in_=x1b[:, :]), deps=[t_snap], dma=True)
        if dev and ub == 0 and os.environ.get("KCUT", "0") == "1":
            with nc.allow_low_precision(reason="dev"):
                P.emit(nc, sems)
            nc._keep = (st,)
            return nc
        t_st = rowstats(xb[pb][:, :], D, EPS_LN, [t_x])
        if dev and ub == 0:
            P.op("sync", lambda e: e.dma_start(out=dbgf[0], in_=statb[:, :]), deps=[t_st], dma=True)
        t_y = P.op("vector", lambda e, pb=pb: e.tensor_scalar(out=yb[:, :], in0=xb[pb][:, :], scalar1=statb[:, 3:4], scalar2=statb[:, 4:5], op0=ALU.mult, op1=ALU.add), deps=[t_st])
        if dev and ub == 0:
            P.op("sync", lambda e: e.dma_start(out=dbgf[1], in_=statb[:, :]), deps=[t_y], dma=True)
            P.op("sync", lambda e: e.dma_start(out=dbgd[11], in_=yb[:, :]), deps=[t_y], dma=True)
        evac_toks = []
        prev_evac = {0: [t_y], 1: [t_y]}
        for g4 in range(4):
            pt = g4 % 2
            trs = []
            for k4 in range(4):
                ch = g4 * 4 + k4
                trs.append(P.op("tensor", lambda e, pt=pt, k4=k4, ch=ch: e.transpose(
                    psT[pt][:, 128 * k4:128 * k4 + 128], yb[:, 128 * ch:128 * ch + 128], ident[:, :]),
                    deps=[t_y, t_id] + prev_evac[pt]))
            tev = P.op("vector", lambda e, pt=pt, g4=g4: e.tensor_copy(out=yT[:, 512 * g4:512 * g4 + 512], in_=psT[pt][:, 0:512]), deps=trs)
            prev_evac[pt] = [tev]
            evac_toks.append(tev)
        tqkv = []
        prev_q_evac = [evac_toks[-1]]
        for g4 in range(4):
            mm_toks = []
            for k4 in range(4):
                G = g4 * 4 + k4
                for si, Wx in ((0, Wq), (1, Wk), (2, Wv)):
                    mm_toks.append(P.op("tensor", lambda e, si=si, Wx=Wx, G=G, k4=k4: e.matmul(
                        psA[:, 512 * si + 128 * k4:512 * si + 128 * k4 + 128],
                        yT[:, 128 * G:128 * G + 128], Wx[:, :],
                        start=True, stop=True), deps=[evac_toks[G // 4]] + prev_q_evac))
            ev = []
            for si, Tx in ((0, Tq), (1, Tk), (2, Tv)):
                ev.append(P.op("vector", lambda e, si=si, Tx=Tx, g4=g4: e.tensor_copy(
                    out=Tx[:, 512 * g4:512 * g4 + 512], in_=psA[:, 512 * si:512 * si + 512]), deps=mm_toks))
            prev_q_evac = ev
            tqkv += ev
        b2toks = []
        prev_pt = {0: tqkv, 1: tqkv}
        pi = 0
        for Tx, dstt in ((Tq, q2t), (Tk, k2t), (Tv, v2t)):
            for ci in range(8):
                o = 8 + ci
                pt = pi % 2
                ttr = P.op("tensor", lambda e, Tx=Tx, o=o, pt=pt: e.transpose(
                    psT[pt][:, 0:128],
                    Tx[:, :].rearrange("p (x o) -> p o x", o=16)[:, o, :],
                    ident[:, :]), deps=tqkv + [t_id] + prev_pt[pt])
                tev = P.op("vector", lambda e, dstt=dstt, ci=ci, pt=pt: e.tensor_copy(
                    out=dstt[:, :].rearrange("g (b s p c) -> g b p s c", b=4, s=4, p=8)[:, :, :, :, ci],
                    in_=psT[pt][:, 0:128].rearrange("g (b p s) -> g b p s", b=4, p=8)), deps=[ttr])
                prev_pt[pt] = [tev]
                b2toks.append(tev)
                pi += 1
        tcb = fill_cb(cb1B, 4, tqkv)
        b1outs = []
        for bat in range(4):
            G0 = bat * 4

            def qv1(a, G0=G0):
                return Tq[:, :].rearrange("p (G h o) -> p G h o", h=8, o=16)[:, G0:G0 + 4, a, 0:8][:, :, None, :].broadcast_to([128, 4, 8, 8])

            def kv1(a, G0=G0):
                return Tk[:, :].rearrange("p (G h o) -> p G h o", h=8, o=16)[:, G0:G0 + 4, :, 0:8]

            def vv1(a, single=False, G0=G0):
                v = Tv[:, :].rearrange("p (G h o) -> p G o h", h=8, o=16)[:, G0:G0 + 4, 0:8, :]
                if single:
                    return v[:, :, :, a:a + 1]
                return v

            def dst1(a, G0=G0):
                return OC[:, :].rearrange("p (g c) -> p g c", c=16).rearrange("p (G h) c -> p G h c", h=8)[:, G0:G0 + 4, a, 0:8]

            b1outs += branch(4, qv1, kv1, vv1, cw1B, cb1B, dst1,
                             tqkv + [tcb] + b1outs[-1:])
        tcb2 = fill_cb(cb2B, 8, b1outs)
        b2outs = []
        for bat in range(2):
            B0 = bat * 2

            def qv2(a, B0=B0):
                return q2t[:, :].rearrange("g (b s p c) -> g b s p c", b=4, s=4, p=8)[:, B0:B0 + 2, :, a, :].rearrange("g b s c -> g (b s) c")[:, :, None, :].broadcast_to([128, 8, 8, 8])

            def kv2(a, B0=B0):
                return k2t[:, :].rearrange("g (b s p c) -> g b s p c", b=4, s=4, p=8)[:, B0:B0 + 2, :, :, :].rearrange("g b s p c -> g (b s) p c")

            def vv2(a, single=False, B0=B0):
                v = v2t[:, :].rearrange("g (b s p c) -> g b s c p", b=4, s=4, p=8)[:, B0:B0 + 2, :, :, :].rearrange("g b s c p -> g (b s) c p")
                if single:
                    return v[:, :, :, a:a + 1]
                return v

            def dst2(a, B0=B0):
                return OC2[:, :].rearrange("g (b p s c) -> g b p s c", b=4, p=8, s=4)[:, B0:B0 + 2, a, :, :]

            b2outs += branch(8, qv2, kv2, vv2, cw2B, cb2B, dst2,
                             b2toks + [tcb2] + b2outs[-1:])
        prev_bt = {0: b2outs, 1: b2outs}
        btoks = []
        for ci in range(8):
            pt = ci % 2
            ttr = P.op("tensor", lambda e, ci=ci, pt=pt: e.transpose(
                psT[pt][:, 0:128],
                OC2[:, :].rearrange("g (r c) -> g c r", c=8)[:, ci, :],
                ident[:, :]), deps=b2outs + [t_id] + prev_bt[pt])
            tev = P.op("vector", lambda e, ci=ci, pt=pt: e.tensor_copy(
                out=OC[:, :].rearrange("p (g c) -> p c g", c=16)[:, 8 + ci, :], in_=psT[pt][:, 0:128]), deps=[ttr])
            prev_bt[pt] = [tev]
            btoks.append(tev)
        oc_done = b1outs + btoks
        proj_toks = []
        prev_ot = oc_done
        for G in range(16):
            ttr = P.op("tensor", lambda e, G=G: e.transpose(
                psT[0][:, 0:128],
                OC[:, 128 * G:128 * G + 128],
                ident[:, :]), deps=prev_ot + [t_id])
            tev = P.op("vector", lambda e: e.tensor_copy(out=OTg[:, :], in_=psT[0][:, 0:128]), deps=[ttr])
            tmm = P.op("tensor", lambda e, G=G: e.matmul(
                psA[:, 128 * G:128 * G + 128],
                OTg[:, :], Pbd[:, :], start=True, stop=True), deps=[tev])
            prev_ot = [tmm]
            proj_toks.append(tmm)
        evo = []
        for o in range(16):
            evo.append(P.op("vector", lambda e, o=o: e.tensor_scalar(
                out=OCP[:, o * 128:(o + 1) * 128],
                in0=psA[:, :].rearrange("p (G h o) -> p o G h", h=8, o=16)[:, o, :, :],
                scalar1=pbB[:, o:o + 1], scalar2=None, op0=ALU.add), deps=proj_toks))
        t_mu = P.op("vector", lambda e: e.tensor_reduce(
            out=mug[:, :], in_=OCP[:, :].rearrange("p (c g) -> p g c", c=16), axis=AX.X, op=ALU.add), deps=evo)
        t_sq = P.op("scalar", lambda e: e.activation(out=SQb[:, :], in_=OCP[:, :], func=AF.Square), deps=evo)
        t_ms = P.op("vector", lambda e: e.tensor_reduce(
            out=msg[:, :], in_=SQb[:, :].rearrange("p (c g) -> p g c", c=16), axis=AX.X, op=ALU.add), deps=[t_sq])
        t = P.op("vector", lambda e: e.tensor_scalar(out=mug[:, :], in0=mug[:, :], scalar1=float(1.0 / 16), scalar2=None, op0=ALU.mult), deps=[t_mu])
        t = P.op("vector", lambda e: e.tensor_scalar(out=msg[:, :], in0=msg[:, :], scalar1=float(1.0 / 16), scalar2=None, op0=ALU.mult), deps=[t_ms, t])
        t = P.op("vector", lambda e: e.tensor_tensor(out=rsg[:, :], in0=mug[:, :], in1=mug[:, :], op=ALU.mult), deps=[t])
        t = P.op("vector", lambda e: e.tensor_tensor(out=rsg[:, :], in0=msg[:, :], in1=rsg[:, :], op=ALU.subtract), deps=[t])
        t = P.op("vector", lambda e: e.tensor_scalar(out=rsg[:, :], in0=rsg[:, :], scalar1=EPS_LN, scalar2=None, op0=ALU.add), deps=[t])
        t = P.op("scalar", lambda e: e.activation(out=rsg[:, :], in_=rsg[:, :], func=AF.Ln), deps=[t])
        t = P.op("scalar", lambda e: e.activation(out=rsg[:, :], in_=rsg[:, :], func=AF.Exp, scale=-0.5), deps=[t])
        t = P.op("vector", lambda e: e.tensor_tensor(out=m2g[:, :], in0=mug[:, :], in1=rsg[:, :], op=ALU.mult), deps=[t])
        t_ln = P.op("vector", lambda e: e.tensor_scalar(out=m2g[:, :], in0=m2g[:, :], scalar1=-1.0, scalar2=None, op0=ALU.mult), deps=[t])
        nf_toks = []
        for c in range(16):
            t1 = P.op("vector", lambda e, c=c: e.scalar_tensor_tensor(
                out=NF[:, c * 128:(c + 1) * 128], in0=OCP[:, c * 128:(c + 1) * 128],
                scalar=lnwB[:, c:c + 1], in1=rsg[:, :], op0=ALU.mult, op1=ALU.mult), deps=[t_ln])
            t2 = P.op("vector", lambda e, c=c: e.scalar_tensor_tensor(
                out=junk[:, 0:128], in0=m2g[:, :], scalar=lnwB[:, c:c + 1],
                in1=NF[:, c * 128:(c + 1) * 128], op0=ALU.mult, op1=ALU.add), deps=[t1])
            t3 = P.op("vector", lambda e, c=c: e.scalar_tensor_tensor(
                out=NF[:, c * 128:(c + 1) * 128],
                in0=yb[:, :].rearrange("p (g c) -> p c g", c=16)[:, c, :],
                scalar=lnbB[:, c:c + 1], in1=junk[:, 0:128], op0=ALU.add, op1=ALU.add), deps=[t2])
            nf_toks.append(t3)
        if dev and ub == 0:
            t_sn2 = P.op("vector", lambda e, pb=pb: e.tensor_copy(out=h2b[:, :], in_=xb[pb][:, :]), deps=nf_toks)
            P.op("sync", lambda e: e.dma_start(out=dbgd[9], in_=h2b[:, :]), deps=[t_sn2], dma=True)
        if dev and ub == 0:
            for di, buf in enumerate((yb, yT, Tq, Tk, OC, OCP, NF)):
                P.op("sync", lambda e, di=di, buf=buf: e.dma_start(out=dbgd[di], in_=buf[:, :]), deps=nf_toks, dma=True)
            P.op("sync", lambda e: e.dma_start(out=dbgd[7][:, 0:1024], in_=OC2[:, :]), deps=nf_toks, dma=True)
        m_base = 8 * ub
        sc_toks = []
        for beta in range(4):
            for phi in range(2):
                mval = m_base + 2 * beta + phi
                sc_toks.append(P.op("sync", lambda e, beta=beta, phi=phi, mval=mval: e.dma_start(
                    out=attd.rearrange("(c m) (t g) -> m t c g", m=256, g=128)[mval],
                    in_=NF[32 * beta + 16 * phi:32 * beta + 16 * phi + 16, :].rearrange("p (c g) -> p c g", c=16)),
                    deps=nf_toks, dma=True))
        scatter_toks[ub // 16] += sc_toks
        prev_sb_done = nf_toks + sc_toks

    P1_DONE = P.barrier()
    st1.close()
    if KTRUNC:
        with nc.allow_low_precision(reason="bf16 stream"):
            P.emit(nc, sems)
        nc._keep = (st, stA)
        return nc

    WKsb = sb("WKsb", [128, 16 * 512], BF16, stpc)
    WVsb = sb("WVsb", [128, 16 * 512], BF16, stpc)

    attc = sb("attc", [128, D], BF16, stpc)
    x2t = sb("x2t", [128, D], BF16, stpc)
    ht = sb("ht", [128, D], BF16, stpc)
    hT = sb("hT", [128, D], BF16, stpc)
    ktmp = sb("ktmp", [128, 512], BF16, stpc)
    ssr = rowbuf

    P.fence(P1_DONE)
    twv1 = P.op("gpsimd", lambda e: e.dma_start(out=WKsb[:, :].rearrange("p (c f) -> p c f", c=16), in_=w["ffn_wk"].rearrange("(c p) f -> p c f", p=128)), deps=P1_DONE, dma=True)
    twv2 = P.op("gpsimd", lambda e: e.dma_start(out=WVsb[:, :].rearrange("p (c f) -> p c f", c=16), in_=w["ffn_wv"].rearrange("(c p) f -> p c f", p=128)), deps=P1_DONE, dma=True)
    pc_prev = P1_DONE
    kv_done = []
    for T in range(NT):
        r0 = 128 * T
        half = T % 2
        dep0 = pc_prev + scatter_toks[half]
        t_a = P.op("sync", lambda e, r0=r0: e.dma_start(out=attc[:, :], in_=attd[r0:r0 + 128, :]), deps=dep0, dma=True)
        t_x = P.op("sync", lambda e, r0=r0: e.dma_start(out=xb[0][:, :], in_=xd[r0:r0 + 128, :]), deps=dep0, dma=True)
        t_st = rowstats(attc[:, :], D, EPS_LN, [t_a])
        t1 = P.op("vector", lambda e: e.tensor_scalar(out=junk[:, :], in0=attc[:, :], scalar1=statb[:, 3:4], scalar2=statb[:, 4:5], op0=ALU.mult, op1=ALU.add), deps=[t_st])
        t_x2 = P.op("vector", lambda e: e.tensor_tensor(out=x2t[:, :], in0=junk[:, :], in1=xb[0][:, :], op=ALU.add), deps=[t1, t_x])
        t_sp = P.op("sync", lambda e, r0=r0: e.dma_start(out=x2d[r0:r0 + 128, :], in_=x2t[:, :]), deps=[t_x2], dma=True)
        t_st2 = rowstats(x2t[:, :], D, EPS_LN, [t_x2])
        t2 = P.op("vector", lambda e: e.tensor_scalar(out=junk[:, :], in0=x2t[:, :], scalar1=statb[:, 3:4], scalar2=statb[:, 4:5], op0=ALU.mult, op1=ALU.add), deps=[t_st2])
        t_rs = rmsstats(junk[:, :], D, EPS_RMS, [t2])
        t3 = P.op("vector", lambda e: e.tensor_scalar(out=ht[:, :], in0=junk[:, :], scalar1=statb[:, 6:7], scalar2=None, op0=ALU.mult), deps=[t_rs])
        t_h = P.op("vector", lambda e: e.tensor_tensor(out=ht[:, :], in0=ht[:, :], in1=LN1W[:, :], op=ALU.mult), deps=[t3])
        evh = []
        prev_e = {0: [t_h], 1: [t_h]}
        for g4 in range(4):
            pt = g4 % 2
            trs = []
            for k4 in range(4):
                ch = g4 * 4 + k4
                trs.append(P.op("tensor", lambda e, pt=pt, k4=k4, ch=ch: e.transpose(
                    psT[pt][:, 128 * k4:128 * k4 + 128], ht[:, 128 * ch:128 * ch + 128], ident[:, :]),
                    deps=[t_h, t_id] + prev_e[pt]))
            tev = P.op("vector", lambda e, pt=pt, g4=g4: e.tensor_copy(out=hT[:, 512 * g4:512 * g4 + 512], in_=psT[pt][:, 0:512]), deps=trs)
            prev_e[pt] = [tev]
            evh.append(tev)
        kmms = []
        for ch in range(16):
            kmms.append(P.op("tensor", lambda e, ch=ch: e.matmul(
                psA[:, 0:512], hT[:, 128 * ch:128 * ch + 128],
                WKsb[:, ch * 512:(ch + 1) * 512],
                start=(ch == 0), stop=(ch == 15)), deps=[evh[ch // 4]] + kv_done))
        vmms = []
        for ch in range(16):
            vmms.append(P.op("tensor", lambda e, ch=ch: e.matmul(
                psA[:, 512:1024], hT[:, 128 * ch:128 * ch + 128],
                WVsb[:, ch * 512:(ch + 1) * 512],
                start=(ch == 0), stop=(ch == 15)), deps=[evh[ch // 4]] + kv_done))
        t_rs2 = rmsstats(psA[:, 0:512], 512, EPS_RMS, kmms, col=7)
        t_ke = P.op("vector", lambda e: e.tensor_scalar(out=ktmp[:, :], in0=psA[:, 0:512], scalar1=statb[:, 7:8], scalar2=None, op0=ALU.mult), deps=[t_rs2] + kmms)
        t_kn = P.op("vector", lambda e: e.tensor_tensor(out=ktmp[:, :], in0=ktmp[:, :], in1=KNWB[:, :], op=ALU.mult), deps=[t_ke])
        t_pi = P.op("gpsimd", lambda e, r0=r0: e.iota(i32a[:, 33:34], pattern=[[0, 1]], base=r0, channel_multiplier=1), deps=[t_kn] + kv_done)
        t_pb = P.op("vector", lambda e: e.tensor_copy(out=posb[:, :], in_=i32a[:, 33:34]), deps=[t_pi])
        t_s, t_c = slot_tables(posb[:, :], [t_pb])
        rp = [t_s, t_c, t_kn]
        for kvh in range(4):
            c0 = 128 * kvh
            tA = P.op("vector", lambda e, c0=c0: e.tensor_tensor(out=rtp[:, 0:32], in0=ktmp[:, c0:c0 + 32], in1=cqt[:, :], op=ALU.mult), deps=rp)
            tB = P.op("vector", lambda e, c0=c0: e.tensor_tensor(out=rtp[:, 32:64], in0=ktmp[:, c0 + 32:c0 + 64], in1=sqt[:, :], op=ALU.mult), deps=rp)
            tC = P.op("vector", lambda e, c0=c0: e.tensor_tensor(out=rtp[:, 64:96], in0=ktmp[:, c0 + 32:c0 + 64], in1=cqt[:, :], op=ALU.mult), deps=rp)
            tD = P.op("vector", lambda e, c0=c0: e.tensor_tensor(out=rtp[:, 96:128], in0=ktmp[:, c0:c0 + 32], in1=sqt[:, :], op=ALU.mult), deps=rp)
            t1 = P.op("vector", lambda e, c0=c0: e.tensor_tensor(out=ktmp[:, c0:c0 + 32], in0=rtp[:, 0:32], in1=rtp[:, 32:64], op=ALU.subtract), deps=[tA, tB, tC, tD])
            rp = [P.op("vector", lambda e, c0=c0: e.tensor_tensor(out=ktmp[:, c0 + 32:c0 + 64], in0=rtp[:, 64:96], in1=rtp[:, 96:128], op=ALU.add), deps=[t1]), t_s, t_c]
        pt = T % 2
        ktr = []
        for kvh in range(4):
            ktr.append(P.op("tensor", lambda e, kvh=kvh, pt=pt: e.transpose(
                psT[pt][:, 128 * kvh:128 * kvh + 128], ktmp[:, 128 * kvh:128 * kvh + 128], ident[:, :]),
                deps=rp + [t_id] + kv_done))
        kst = []
        for kvh in range(4):
            kst.append(P.op("vector", lambda e, kvh=kvh, pt=pt, r0=r0: e.tensor_copy(
                out=KT[:, kvh * S + r0:kvh * S + r0 + 128], in_=psT[pt][:, 128 * kvh:128 * kvh + 128]), deps=ktr))
        t_ks = kst[-1]
        t_vs = P.op("vector", lambda e, T=T: e.tensor_copy(out=VS[:, 512 * T:512 * T + 512], in_=psA[:, 512:1024]), deps=vmms)
        kv_done = kst + [t_vs, t_ke]
        pc_prev = [t_ks, t_vs, t_sp, t_x2]

    PC_DONE = P.barrier()
    stpc.close()
    P.fence(PC_DONE)

    qT = sb("qT", [128, 16 * 16 * 128], BF16, stq)
    bndS = sb("bndS", [128, 16 * 16], F32, stat2)
    MnegS = sb("MnegS", [128, 16 * 256], BF16, stat2)
    KmxB = sb("KmxB", [128, 1], F32, stat2)
    Astr = sb("Astr", [128, 32 * 128], BF16, stat2)
    ATb = [sb("ATb0", [128, 128], BF16, stat2), sb("ATb1", [128, 128], BF16, stat2)]
    dnmb = sb("dnmb", [128, 8], F32, stat2)
    rcpb = sb("rcpb", [128, 1], F32, stat2)
    otb = sb("otb", [128, 128], BF16, stat2)

    hoT = [sb("hoT%d" % k, [128, D], BF16, st2a) for k in range(2)]
    qrawS = [sb("qrawS%d" % k, [128, D], BF16, st2a) for k in range(2)]
    qhn = sb("qhn", [128, 16], F32, st2a)
    wqb = [sb("wqb0", [128, 512], BF16, st2a), sb("wqb1", [128, 512], BF16, st2a)]
    qsb = sb("qsb", [128, 16], F32, st2a)

    t_kq = P.op("scalar", lambda e: e.activation(out=junk[:, 0:2048], in_=KT[:, 0:2048], func=AF.Square), deps=PC_DONE)
    tmx = []
    t_mxacc = None
    for blk in range(KV * S // 2048):
        t_kq2 = P.op("scalar", lambda e, blk=blk: e.activation(out=junk[:, 0:2048], in_=KT[:, 2048 * blk:2048 * blk + 2048], func=AF.Square), deps=PC_DONE + tmx) if blk > 0 else t_kq
        t_m1 = P.op("tensor", lambda e: e.matmul(psC[0:1, 0:512], onesc[:, :], junk[:, 0:512], start=True, stop=True), deps=[t_kq2, t_onesc] + tmx)
        toksb = [t_m1]
        for sub in range(1, 4):
            toksb.append(P.op("tensor", lambda e, sub=sub: e.matmul(
                psC[0:1, 512:1024], onesc[:, :], junk[:, 512 * sub:512 * sub + 512],
                start=(sub == 1), stop=(sub == 3)), deps=[t_kq2] + toksb))
        t_r1 = P.op("vector", lambda e: e.tensor_reduce(out=ssr[0:1, 0:1], in_=psC[0:1, 0:512], axis=AX.X, op=ALU.max), deps=toksb)
        t_r2 = P.op("vector", lambda e: e.tensor_reduce(out=ssr[0:1, 1:2], in_=psC[0:1, 512:1024], axis=AX.X, op=ALU.max), deps=[t_r1])
        if t_mxacc is None:
            t_mxacc = P.op("vector", lambda e: e.tensor_reduce(out=ssr[0:1, 4:5], in_=ssr[0:1, 0:2], axis=AX.X, op=ALU.max), deps=[t_r2])
        else:
            t_m3 = P.op("vector", lambda e: e.tensor_reduce(out=ssr[0:1, 2:3], in_=ssr[0:1, 0:2], axis=AX.X, op=ALU.max), deps=[t_r2])
            t_mxacc = P.op("vector", lambda e: e.tensor_tensor(out=ssr[0:1, 4:5], in0=ssr[0:1, 4:5], in1=ssr[0:1, 2:3], op=ALU.max), deps=[t_m3, t_mxacc])
        tmx = [t_mxacc]
    t = P.op("scalar", lambda e: e.activation(out=ssr[0:1, 4:5], in_=ssr[0:1, 4:5], func=AF.Sqrt), deps=tmx)
    t = P.op("vector", lambda e: e.tensor_scalar(out=ssr[0:1, 4:5], in0=ssr[0:1, 4:5], scalar1=-SCALE_F, scalar2=None, op0=ALU.mult), deps=[t])
    t = P.op("tensor", lambda e: e.matmul(psC[:, 512:513], ones1[0:1, :], ssr[0:1, 4:5], start=True, stop=True), deps=[t, t_ones1])
    t_kmx = P.op("vector", lambda e: e.tensor_copy(out=KmxB[:, :], in_=psC[:, 512:513]), deps=[t])

    q_done = []
    for grp in range(8):
        gtoks = []
        for sl in range(2):
            i = grp * 2 + sl
            t_g = P.op("gpsimd", lambda e, i=i: e.indirect_dma_start(
                out=hob[:, :], out_offset=None, in_=x2d,
                in_offset=bass.IndirectOffsetOnAxis(idxS[:, i:i + 1], 0)),
                deps=PC_DONE + q_done + gtoks, dma=True)
            t_st = rowstats(hob[:, :], D, EPS_LN, [t_g])
            t1 = P.op("vector", lambda e, sl=sl: e.tensor_scalar(out=qrawS[sl][:, :], in0=hob[:, :], scalar1=statb[:, 3:4], scalar2=statb[:, 4:5], op0=ALU.mult, op1=ALU.add), deps=[t_st])
            t_rs = rmsstats(qrawS[sl][:, :], D, EPS_RMS, [t1])
            t2 = P.op("vector", lambda e, sl=sl: e.tensor_scalar(out=qrawS[sl][:, :], in0=qrawS[sl][:, :], scalar1=statb[:, 6:7], scalar2=None, op0=ALU.mult), deps=[t_rs])
            t_h = P.op("vector", lambda e, sl=sl: e.tensor_tensor(out=hob[:, :], in0=qrawS[sl][:, :], in1=LN1W[:, :], op=ALU.mult), deps=[t2])
            prev_e = {0: [t_h], 1: [t_h]}
            for g4 in range(4):
                pt = g4 % 2
                trs = []
                for k4 in range(4):
                    ch = g4 * 4 + k4
                    trs.append(P.op("tensor", lambda e, pt=pt, k4=k4, ch=ch: e.transpose(
                        psT[pt][:, 128 * k4:128 * k4 + 128], hob[:, 128 * ch:128 * ch + 128], ident[:, :]),
                        deps=[t_h, t_id] + prev_e[pt]))
                tev = P.op("vector", lambda e, pt=pt, g4=g4, sl=sl: e.tensor_copy(out=hoT[sl][:, 512 * g4:512 * g4 + 512], in_=psT[pt][:, 0:512]), deps=trs)
                prev_e[pt] = [tev]
                gtoks.append(tev)
            cap = CAPS[i]
            t_mi = P.op("gpsimd", lambda e, cap=cap: e.iota(i32a[:, 0:128], pattern=[[1, 128]], base=128 * (cap - 2), channel_multiplier=0), deps=PC_DONE + q_done + gtoks)
            t_mi = P.op("vector", lambda e: e.tensor_copy(out=rtp[:, :], in_=i32a[:, 0:128]), deps=[t_mi])
            t_mm1 = P.op("vector", lambda e, i=i: e.tensor_scalar(
                out=MnegS[:, 256 * i:256 * i + 128], in0=rtp[:, :],
                scalar1=sposS[:, i:i + 1], scalar2=NEG, op0=ALU.is_gt, op1=ALU.mult), deps=[t_mi])
            t_mi2 = P.op("gpsimd", lambda e, cap=cap: e.iota(i32a[:, 0:128], pattern=[[1, 128]], base=128 * (cap - 1), channel_multiplier=0), deps=[t_mi, t_mm1])
            t_mi2b = P.op("vector", lambda e: e.tensor_copy(out=rtp[:, :], in_=i32a[:, 0:128]), deps=[t_mi2])
            t_mm2 = P.op("vector", lambda e, i=i: e.tensor_scalar(
                out=MnegS[:, 256 * i + 128:256 * i + 256], in0=rtp[:, :],
                scalar1=sposS[:, i:i + 1], scalar2=NEG, op0=ALU.is_gt, op1=ALU.mult), deps=[t_mi2b])
            gtoks += [t_mm1, t_mm2]
        fq_toks = []
        for fq in range(4):
            mmt = []
            prev_ev = list(fq_toks)
            for ch in range(16):
                t_w = P.op("gpsimd", lambda e, ch=ch, fq=fq: e.dma_start(
                    out=wqb[ch % 2][:, :], in_=w["ffn_wq"][128 * ch:128 * ch + 128, 512 * fq:512 * fq + 512]),
                    deps=gtoks + mmt[-8:], dma=True)
                for sl in range(2):
                    mmt.append(P.op("tensor", lambda e, sl=sl, ch=ch: e.matmul(
                        psA[:, 512 * sl:512 * sl + 512], hoT[sl][:, 128 * ch:128 * ch + 128],
                        wqb[ch % 2][:, :], start=(ch == 0), stop=(ch == 15)),
                        deps=[t_w] + gtoks + (prev_ev if ch == 0 else [])))
            evs = []
            for sl in range(2):
                i = grp * 2 + sl
                evs.append(P.op("vector", lambda e, sl=sl, fq=fq, i=i: e.tensor_copy(
                    out=qrawS[sl][:, 512 * fq:512 * fq + 512], in_=psA[:, 512 * sl:512 * sl + 512]), deps=mmt))
                evs.append(P.op("scalar", lambda e, sl=sl, fq=fq: e.activation(
                    out=junk[:, 0:512], in_=psA[:, 512 * sl:512 * sl + 512], func=AF.Square,
                    accum_out=qsb[:, 4 * sl + fq:4 * sl + fq + 1]), deps=mmt + evs[-1:]))
            fq_toks = evs
        sl_toks = []
        for sl in range(2):
            i = grp * 2 + sl
            t = P.op("vector", lambda e, sl=sl: e.tensor_reduce(out=statb[:, 8:9], in_=qsb[:, 4 * sl:4 * sl + 4], axis=AX.X, op=ALU.add), deps=fq_toks + sl_toks)
            t = P.op("vector", lambda e: e.tensor_scalar(out=statb[:, 8:9], in0=statb[:, 8:9], scalar1=float(1.0 / D), scalar2=EPS_RMS, op0=ALU.mult, op1=ALU.add), deps=[t])
            t = P.op("scalar", lambda e: e.activation(out=statb[:, 8:9], in_=statb[:, 8:9], func=AF.Ln), deps=[t])
            t_rq = P.op("scalar", lambda e: e.activation(out=statb[:, 8:9], in_=statb[:, 8:9], func=AF.Exp, scale=-0.5), deps=[t])
            t = P.op("vector", lambda e, sl=sl: e.tensor_scalar(out=qrawS[sl][:, :], in0=qrawS[sl][:, :], scalar1=statb[:, 8:9], scalar2=None, op0=ALU.mult), deps=[t_rq])
            t_qn = P.op("vector", lambda e, sl=sl: e.tensor_tensor(out=qrawS[sl][:, :], in0=qrawS[sl][:, :], in1=QNW[:, :], op=ALU.mult), deps=[t])
            t_sq = P.op("scalar", lambda e, sl=sl: e.activation(out=SQb[:, :], in_=qrawS[sl][:, :], func=AF.Square), deps=[t_qn])
            t_hn = P.op("vector", lambda e: e.tensor_reduce(out=qhn[:, :], in_=SQb[:, :].rearrange("p (h d) -> p h d", h=16), axis=AX.X, op=ALU.add), deps=[t_sq])
            t_hn = P.op("scalar", lambda e: e.activation(out=qhn[:, :], in_=qhn[:, :], func=AF.Sqrt), deps=[t_hn])
            t_bd = P.op("vector", lambda e, i=i: e.tensor_scalar(out=bndS[:, 16 * i:16 * i + 16], in0=qhn[:, :], scalar1=KmxB[:, :], scalar2=None, op0=ALU.mult), deps=[t_hn, t_kmx])
            t_s, t_c = slot_tables(sposS[:, i:i + 1], [t_bd])
            rp = [t_c, t_s, t_qn]
            for hh in range(16):
                c0 = 128 * hh
                tA = P.op("vector", lambda e, c0=c0, sl=sl: e.tensor_tensor(out=rtp[:, 0:32], in0=qrawS[sl][:, c0:c0 + 32], in1=cqt[:, :], op=ALU.mult), deps=rp)
                tB = P.op("vector", lambda e, c0=c0, sl=sl: e.tensor_tensor(out=rtp[:, 32:64], in0=qrawS[sl][:, c0 + 32:c0 + 64], in1=sqt[:, :], op=ALU.mult), deps=rp)
                tC = P.op("vector", lambda e, c0=c0, sl=sl: e.tensor_tensor(out=rtp[:, 64:96], in0=qrawS[sl][:, c0 + 32:c0 + 64], in1=cqt[:, :], op=ALU.mult), deps=rp)
                tD = P.op("vector", lambda e, c0=c0, sl=sl: e.tensor_tensor(out=rtp[:, 96:128], in0=qrawS[sl][:, c0:c0 + 32], in1=sqt[:, :], op=ALU.mult), deps=rp)
                t1 = P.op("vector", lambda e, c0=c0, sl=sl: e.tensor_tensor(out=qrawS[sl][:, c0:c0 + 32], in0=rtp[:, 0:32], in1=rtp[:, 32:64], op=ALU.subtract), deps=[tA, tB, tC, tD])
                rp = [P.op("vector", lambda e, c0=c0, sl=sl: e.tensor_tensor(out=qrawS[sl][:, c0 + 32:c0 + 64], in0=rtp[:, 64:96], in1=rtp[:, 96:128], op=ALU.add), deps=[t1])]
                rp += [t_c]
            prev_e = {0: rp, 1: rp}
            for g4 in range(4):
                pt = g4 % 2
                trs = []
                for k4 in range(4):
                    hh = g4 * 4 + k4
                    trs.append(P.op("tensor", lambda e, pt=pt, k4=k4, hh=hh, sl=sl: e.transpose(
                        psT[pt][:, 128 * k4:128 * k4 + 128], qrawS[sl][:, 128 * hh:128 * hh + 128], ident[:, :]),
                        deps=rp + [t_id] + prev_e[pt]))
                tev = P.op("vector", lambda e, pt=pt, g4=g4, i=i: e.tensor_copy(
                    out=qT[:, :].rearrange("p (h i q) -> p h i q", h=16, i=16)[:, 4 * g4:4 * g4 + 4, i, :],
                    in_=psT[pt][:, 0:512].rearrange("p (k q) -> p k q", k=4)), deps=trs)
                prev_e[pt] = [tev]
                sl_toks.append(tev)
        q_done = sl_toks + fq_toks

    P2A_DONE = P.barrier()
    st2a.close()
    P.fence(P2A_DONE)


    prev_oe = []
    prev_oed = []
    prev_exp = {0: [], 1: []}
    for hh in range(16):
        kvh = hh // 4
        for i in range(16):
            cap = CAPS[i]
            kvlen = cap * 128
            qcol = (hh * 16 + i) * 128
            exp_toks = []
            n512 = (kvlen + 511) // 512
            for ck in range(n512):
                c0 = ck * 512
                cw = min(512, kvlen - c0)
                pg = ck % 2
                mm = P.op("tensor", lambda e, pg=pg, cw=cw, qcol=qcol, kvh=kvh, c0=c0: e.matmul(
                    psA[:, 512 * pg:512 * pg + cw], qT[:, qcol:qcol + 128],
                    KT[:, kvh * S + c0:kvh * S + c0 + cw], start=True, stop=True),
                    deps=prev_exp[pg] + prev_oe)
                mtoks = [mm]
                if c0 + cw > kvlen - 256:
                    s0 = max(c0, kvlen - 256)
                    moff = s0 - (kvlen - 256)
                    mlen = c0 + cw - s0
                    mtoks.append(P.op("vector", lambda e, pg=pg, s0=s0, c0=c0, moff=moff, mlen=mlen, i=i: e.tensor_tensor(
                        out=psA[:, 512 * pg + (s0 - c0):512 * pg + (s0 - c0) + mlen],
                        in0=psA[:, 512 * pg + (s0 - c0):512 * pg + (s0 - c0) + mlen],
                        in1=MnegS[:, 256 * i + moff:256 * i + moff + mlen], op=ALU.add), deps=[mm]))
                texp = P.op("scalar", lambda e, pg=pg, cw=cw, c0=c0, ck=ck, i=i, hh=hh: e.activation(
                    out=Astr[:, c0:c0 + cw], in_=psA[:, 512 * pg:512 * pg + cw], func=AF.Exp,
                    bias=bndS[:, 16 * i + hh:16 * i + hh + 1], scale=SCALE_F,
                    accum_out=dnmb[:, ck:ck + 1]), deps=mtoks)
                exp_toks.append(texp)
                prev_exp[pg] = [texp]
            t_d = P.op("vector", lambda e, n512=n512: e.tensor_reduce(out=rcpb[:, :], in_=dnmb[:, 0:n512], axis=AX.X, op=ALU.add), deps=exp_toks)
            t_d = P.op("vector", lambda e: e.reciprocal(out=rcpb[:, :], in_=rcpb[:, :]), deps=[t_d])
            t_rs = P.op("vector", lambda e, kvlen=kvlen: e.tensor_scalar(
                out=Astr[:, 0:kvlen], in0=Astr[:, 0:kvlen], scalar1=rcpb[:, :], scalar2=None, op0=ALU.mult), deps=[t_d])
            prev_at = {0: [t_rs], 1: [t_rs]}
            avmms = []
            for kt in range(cap):
                pt = kt % 2
                ttr = P.op("tensor", lambda e, pt=pt, kt=kt: e.transpose(
                    psT[pt][:, 0:128], Astr[:, kt * 128:kt * 128 + 128], ident[:, :]),
                    deps=[t_rs, t_id] + prev_at[pt])
                tev = P.op("vector", lambda e, pt=pt: e.tensor_copy(out=ATb[pt][:, :], in_=psT[pt][:, 0:128]), deps=[ttr])
                avmms.append(P.op("tensor", lambda e, pt=pt, kt=kt, kvh=kvh, cap=cap: e.matmul(
                    psC[:, 0:128], VS[:, kt * 512 + kvh * 128:kt * 512 + kvh * 128 + 128],
                    ATb[pt][:, :], start=(kt == 0), stop=(kt == cap - 1)),
                    deps=[tev] + (prev_oe if kt == 0 else [])))
                prev_at[pt] = [avmms[-1]]
            t_oe = P.op("vector", lambda e: e.tensor_copy(out=otb[:, :], in_=psC[:, 0:128]), deps=avmms + prev_oed)
            t_od = P.op("sync", lambda e, hh=hh, i=i: e.dma_start(
                out=otd[128 * hh:128 * hh + 128, 128 * i:128 * i + 128], in_=otb[:, :]), deps=[t_oe], dma=True)
            prev_oe = [t_oe]
            prev_oed = [t_od]

    P2B_DONE = P.barrier()
    stat2.close()
    stq.close()
    stA.close()
    P.fence(P2B_DONE)

    acc = sb("acc", [128, NOWN * D], BF16, stacc)
    cmbS = sb("cmbS", [128, NOWN * E], F32, stmoe)
    WOsb = sb("WOsb", [128, 16 * D], BF16, st2c)
    OTs = sb("OTs", [128, 16 * 128], BF16, st2c)
    h2T = sb("h2T", [128, D], BF16, st2c)
    sgb = sb("sgb", [128, E], F32, st2c)
    yb8 = sb("yb8", [128, E], F32, st2c)
    selb = sb("selb", [128, E], F32, st2c)
    tmp8 = sb("tmp8", [128, E], F32, st2c)
    iotam = sb("iotam", [128, E], F32, st2c)

    t_wo = []
    for ch in range(16):
        t_wo.append(P.op("gpsimd", lambda e, ch=ch: e.dma_start(
            out=WOsb[:, ch * D:(ch + 1) * D],
            in_=w["ffn_wo"][128 * ch:128 * ch + 128, :]), dma=True))
    t_im = P.op("vector", lambda e: e.tensor_scalar(out=iotam[:, :], in0=iota8[:, :], scalar1=-99.0, scalar2=None, op0=ALU.add))

    p2c_done = []
    for i in range(16):
        t_g = P.op("gpsimd", lambda e, i=i: e.indirect_dma_start(
            out=hob[:, :], out_offset=None, in_=x2d,
            in_offset=bass.IndirectOffsetOnAxis(idxS[:, i:i + 1], 0)), deps=p2c_done, dma=True)
        t_ot = P.op("sync", lambda e, i=i: e.dma_start(
            out=OTs[:, :].rearrange("p (c q) -> p c q", c=16),
            in_=otd.rearrange("(c p) q -> p c q", p=128)[:, :, 128 * i:128 * i + 128]), deps=p2c_done, dma=True)
        t_st = rowstats(hob[:, :], D, EPS_LN, [t_g])
        t_z = P.op("vector", lambda e: e.tensor_scalar(out=junk[:, :], in0=hob[:, :], scalar1=statb[:, 3:4], scalar2=statb[:, 4:5], op0=ALU.mult, op1=ALU.add), deps=[t_st])
        womms = []
        for ch in range(16):
            for dq in range(4):
                womms.append(P.op("tensor", lambda e, ch=ch, dq=dq: e.matmul(
                    psA[:, 512 * dq:512 * dq + 512], OTs[:, ch * 128:(ch + 1) * 128],
                    WOsb[:, ch * D + 512 * dq:ch * D + 512 * dq + 512],
                    start=(ch == 0), stop=(ch == 15)), deps=t_wo + p2c_done + [t_ot]))
        ev1 = []
        for dq in range(4):
            ev1.append(P.op("vector", lambda e, dq=dq: e.tensor_tensor(
                out=x1b[:, 512 * dq:512 * dq + 512], in0=psA[:, 512 * dq:512 * dq + 512],
                in1=junk[:, 512 * dq:512 * dq + 512], op=ALU.add), deps=womms + [t_z]))
        t_sx = P.op("sync", lambda e, i=i: e.dma_start(out=x1d[128 * i:128 * i + 128, :], in_=x1b[:, :]), deps=ev1, dma=True)
        t_rs = rmsstats(x1b[:, :], D, EPS_RMS, ev1)
        t_h2 = P.op("vector", lambda e: e.tensor_scalar(out=h2b[:, :], in0=x1b[:, :], scalar1=statb[:, 6:7], scalar2=None, op0=ALU.mult), deps=[t_rs])
        t_h2 = P.op("vector", lambda e: e.tensor_tensor(out=h2b[:, :], in0=h2b[:, :], in1=LN2W[:, :], op=ALU.mult), deps=[t_h2])
        prev_e = {0: [t_h2], 1: [t_h2]}
        h2evs = []
        for g4 in range(4):
            pt = g4 % 2
            trs = []
            for k4 in range(4):
                ch = g4 * 4 + k4
                trs.append(P.op("tensor", lambda e, pt=pt, k4=k4, ch=ch: e.transpose(
                    psT[pt][:, 128 * k4:128 * k4 + 128], h2b[:, 128 * ch:128 * ch + 128], ident[:, :]),
                    deps=[t_h2, t_id] + prev_e[pt]))
            tev = P.op("vector", lambda e, pt=pt, g4=g4: e.tensor_copy(out=h2T[:, 512 * g4:512 * g4 + 512], in_=psT[pt][:, 0:512]), deps=trs)
            prev_e[pt] = [tev]
            h2evs.append(tev)
        t_sh = P.op("sync", lambda e, i=i: e.dma_start(
            out=h2td.rearrange("(c p) q -> p c q", p=128)[:, :, 128 * i:128 * i + 128],
            in_=h2T[:, :].rearrange("p (c q) -> p c q", c=16)), deps=h2evs, dma=True)
        rmms = []
        for ch in range(16):
            rmms.append(P.op("tensor", lambda e, ch=ch: e.matmul(
                psC[:, 512:520], h2T[:, ch * 128:(ch + 1) * 128], RW[:, ch * E:(ch + 1) * E],
                start=(ch == 0), stop=(ch == 15)), deps=h2evs + p2c_done))
        t_sg = P.op("scalar", lambda e: e.activation(out=sgb[:, :], in_=psC[:, 512:520], func=AF.Sigmoid), deps=rmms)
        t_y8 = P.op("vector", lambda e: e.tensor_tensor(out=yb8[:, :], in0=sgb[:, :], in1=biasB[:, :], op=ALU.add), deps=[t_sg])
        tt = P.op("vector", lambda e: e.tensor_reduce(out=statb[:, 9:10], in_=yb8[:, :], axis=AX.X, op=ALU.max), deps=[t_y8])
        tt = P.op("vector", lambda e: e.tensor_scalar(out=selb[:, :], in0=yb8[:, :], scalar1=statb[:, 9:10], scalar2=None, op0=ALU.is_ge), deps=[tt])
        tt = P.op("vector", lambda e: e.tensor_tensor(out=tmp8[:, :], in0=selb[:, :], in1=iotam[:, :], op=ALU.mult), deps=[tt, t_im])
        tt = P.op("vector", lambda e: e.tensor_reduce(out=statb[:, 10:11], in_=tmp8[:, :], axis=AX.X, op=ALU.min), deps=[tt])
        tt = P.op("vector", lambda e: e.tensor_scalar(out=statb[:, 10:11], in0=statb[:, 10:11], scalar1=99.0, scalar2=None, op0=ALU.add), deps=[tt])
        tt = P.op("vector", lambda e: e.tensor_scalar(out=selb[:, :], in0=iota8[:, :], scalar1=statb[:, 10:11], scalar2=None, op0=ALU.is_equal), deps=[tt])
        tt = P.op("vector", lambda e: e.tensor_tensor(out=tmp8[:, :], in0=sgb[:, :], in1=selb[:, :], op=ALU.mult), deps=[tt])
        t_a1 = P.op("vector", lambda e: e.tensor_reduce(out=statb[:, 11:12], in_=tmp8[:, :], axis=AX.X, op=ALU.add), deps=[tt])
        tt = P.op("vector", lambda e: e.scalar_tensor_tensor(out=yb8[:, :], in0=selb[:, :], scalar=NEG, in1=yb8[:, :], op0=ALU.mult, op1=ALU.add), deps=[t_a1])
        t_s1k = P.op("vector", lambda e: e.tensor_copy(out=tmp8[:, :], in_=selb[:, :]), deps=[tt])
        tt = P.op("vector", lambda e: e.tensor_reduce(out=statb[:, 12:13], in_=yb8[:, :], axis=AX.X, op=ALU.max), deps=[t_s1k])
        tt = P.op("vector", lambda e: e.tensor_scalar(out=selb[:, :], in0=yb8[:, :], scalar1=statb[:, 12:13], scalar2=None, op0=ALU.is_ge), deps=[tt])
        tt = P.op("vector", lambda e: e.tensor_tensor(out=yb8[:, :], in0=selb[:, :], in1=iotam[:, :], op=ALU.mult), deps=[tt])
        tt = P.op("vector", lambda e: e.tensor_reduce(out=statb[:, 13:14], in_=yb8[:, :], axis=AX.X, op=ALU.min), deps=[tt])
        tt = P.op("vector", lambda e: e.tensor_scalar(out=statb[:, 13:14], in0=statb[:, 13:14], scalar1=99.0, scalar2=None, op0=ALU.add), deps=[tt])
        tt = P.op("vector", lambda e: e.tensor_scalar(out=selb[:, :], in0=iota8[:, :], scalar1=statb[:, 13:14], scalar2=None, op0=ALU.is_equal), deps=[tt])
        tt = P.op("vector", lambda e: e.tensor_tensor(out=yb8[:, :], in0=sgb[:, :], in1=selb[:, :], op=ALU.mult), deps=[tt])
        t_a2 = P.op("vector", lambda e: e.tensor_reduce(out=statb[:, 14:15], in_=yb8[:, :], axis=AX.X, op=ALU.add), deps=[tt])
        tt = P.op("vector", lambda e: e.tensor_tensor(out=statb[:, 15:16], in0=statb[:, 11:12], in1=statb[:, 14:15], op=ALU.add), deps=[t_a2])
        tt = P.op("vector", lambda e: e.reciprocal(out=statb[:, 15:16], in_=statb[:, 15:16]), deps=[tt])
        tt = P.op("vector", lambda e: e.tensor_tensor(out=statb[:, 11:12], in0=statb[:, 11:12], in1=statb[:, 15:16], op=ALU.mult), deps=[tt])
        tt = P.op("vector", lambda e: e.tensor_tensor(out=statb[:, 14:15], in0=statb[:, 14:15], in1=statb[:, 15:16], op=ALU.mult), deps=[tt])
        tt = P.op("vector", lambda e, i=i: e.tensor_scalar(out=cmbS[:, E * i:E * i + E], in0=tmp8[:, :], scalar1=statb[:, 11:12], scalar2=None, op0=ALU.mult), deps=[tt])
        tt = P.op("vector", lambda e, i=i: e.scalar_tensor_tensor(out=cmbS[:, E * i:E * i + E], in0=selb[:, :], scalar=statb[:, 14:15], in1=cmbS[:, E * i:E * i + E], op0=ALU.mult, op1=ALU.add), deps=[tt])
        p2c_done = [tt, t_sx, t_sh]

    P2C_DONE = P.barrier()
    st2c.close()
    P.fence(P2C_DONE)

    WGsb = sb("WGsb", [128, 16 * MI], BF16, stmoe)
    WUsb = sb("WUsb", [128, 16 * MI], BF16, stmoe)
    WDsb = sb("WDsb", [128, 4 * D], BF16, stmoe)
    h2ts = sb("h2ts", [128, 16 * 512], BF16, stmoe)
    act = sb("act", [128, 8 * 512], BF16, stmoe)
    sgl = sb("sgl", [128, 512], BF16, stmoe)
    upl = sb("upl", [128, 512], BF16, stmoe)
    dnl = sb("dnl", [128, 512], BF16, stmoe)

    t_z0 = P.op("vector", lambda e: e.memset(acc[:, :], 0.0))
    p3_prev = [t_z0]
    for ex in range(E):
        t_wg = P.op("gpsimd", lambda e, ex=ex: e.dma_start(
            out=WGsb[:, :].rearrange("p (c i) -> p c i", c=16),
            in_=w["ffn_wg"][ex].rearrange("(c p) i -> p c i", p=128)), deps=p3_prev, dma=True)
        t_wu = P.op("gpsimd", lambda e, ex=ex: e.dma_start(
            out=WUsb[:, :].rearrange("p (c i) -> p c i", c=16),
            in_=w["ffn_wu"][ex].rearrange("(c p) i -> p c i", p=128)), deps=p3_prev, dma=True)

        sc_prev = []
        for sc in range(4):
            t_h2 = P.op("sync", lambda e, sc=sc: e.dma_start(
                out=h2ts[:, :].rearrange("p (c q) -> p c q", c=16),
                in_=h2td.rearrange("(c p) q -> p c q", p=128)[:, :, 512 * sc:512 * sc + 512]),
                deps=p3_prev + sc_prev, dma=True)
            act_toks = []
            for it in range(8):
                gm = []
                um = []
                for ch in range(16):
                    gm.append(P.op("tensor", lambda e, ch=ch, it=it: e.matmul(
                        psA[:, 0:512], WGsb[:, ch * MI + it * 128:ch * MI + it * 128 + 128],
                        h2ts[:, ch * 512:(ch + 1) * 512], start=(ch == 0), stop=(ch == 15)),
                        deps=[t_wg, t_h2] + act_toks[-2:]))
                    um.append(P.op("tensor", lambda e, ch=ch, it=it: e.matmul(
                        psA[:, 512:1024], WUsb[:, ch * MI + it * 128:ch * MI + it * 128 + 128],
                        h2ts[:, ch * 512:(ch + 1) * 512], start=(ch == 0), stop=(ch == 15)),
                        deps=[t_wu, t_h2] + act_toks[-2:]))
                t_si = P.op("scalar", lambda e: e.activation(out=sgl[:, :], in_=psA[:, 0:512], func=AF.Silu), deps=gm)
                t_uc = P.op("vector", lambda e: e.tensor_copy(out=upl[:, :], in_=psA[:, 512:1024]), deps=um)
                t_ac = P.op("vector", lambda e, it=it: e.tensor_tensor(
                    out=act[:, 512 * it:512 * it + 512], in0=sgl[:, :], in1=upl[:, :], op=ALU.mult), deps=[t_si, t_uc])
                act_toks.append(t_ac)
                act_toks.append(t_si)
            dn_toks = []
            for ih in range(2):
                t_wdh = [P.op("gpsimd", lambda e, ex=ex, ih=ih: e.dma_start(
                    out=WDsb[:, :].rearrange("p (c f) -> p c f", c=4),
                    in_=w["ffn_wd"][ex, 512 * ih:512 * ih + 512].rearrange("(c p) f -> p c f", p=128)),
                    deps=p3_prev + dn_toks[-8:] + act_toks[-2:], dma=True)]
                for stl in range(4):
                    slot = sc * 4 + stl
                    for dq in range(4):
                        pg = dq % 2
                        dm = []
                        for ic4 in range(4):
                            ic = ih * 4 + ic4
                            dm.append(P.op("tensor", lambda e, ic=ic, ic4=ic4, stl=stl, dq=dq, pg=pg: e.matmul(
                                psC[:, 512 * pg:512 * pg + 512],
                                act[:, 512 * ic + 128 * stl:512 * ic + 128 * stl + 128],
                                WDsb[:, ic4 * D + 512 * dq:ic4 * D + 512 * dq + 512],
                                start=(ic4 == 0), stop=(ic4 == 3)),
                                deps=act_toks + dn_toks[-4:] + t_wdh))
                        t_cp = P.op("scalar", lambda e, pg=pg: e.copy(out=dnl[:, :], in_=psC[:, 512 * pg:512 * pg + 512]), deps=dm)
                        t_up = P.op("vector", lambda e, slot=slot, dq=dq, ex=ex: e.scalar_tensor_tensor(
                            out=acc[:, slot * D + 512 * dq:slot * D + 512 * dq + 512],
                            in0=dnl[:, :], scalar=cmbS[:, E * slot + ex:E * slot + ex + 1],
                            in1=acc[:, slot * D + 512 * dq:slot * D + 512 * dq + 512],
                            op0=ALU.mult, op1=ALU.add), deps=[t_cp, t_z0])
                        dn_toks += [t_cp, t_up]
            sc_prev = dn_toks[-8:] + act_toks[-2:]
        p3_prev = sc_prev

    P3_DONE = P.barrier()
    stmoe.close()
    P.fence(P3_DONE)

    ob = sb("ob", [128, D], F32, stp4)
    p4_prev = []
    for i in range(16):
        t_x1 = P.op("sync", lambda e, i=i: e.dma_start(out=x1b[:, :], in_=x1d[128 * i:128 * i + 128, :]), deps=p4_prev, dma=True)
        t_g = P.op("gpsimd", lambda e, i=i: e.indirect_dma_start(
            out=hob[:, :], out_offset=None, in_=x2d,
            in_offset=bass.IndirectOffsetOnAxis(idxS[:, i:i + 1], 0)), deps=p4_prev, dma=True)
        t_f = P.op("vector", lambda e, i=i: e.tensor_tensor(
            out=h2b[:, :], in0=x1b[:, :], in1=acc[:, i * D:(i + 1) * D], op=ALU.add), deps=[t_x1])
        t_st = rowstats(h2b[:, :], D, EPS_LN, [t_f])
        t_n = P.op("vector", lambda e: e.tensor_scalar(out=junk[:, :], in0=h2b[:, :], scalar1=statb[:, 3:4], scalar2=statb[:, 4:5], op0=ALU.mult, op1=ALU.add), deps=[t_st])
        t_o = P.op("vector", lambda e: e.tensor_tensor(out=ob[:, :], in0=junk[:, :], in1=hob[:, :], op=ALU.add), deps=[t_n, t_g])
        t_w = P.op("sync", lambda e, i=i: e.dma_start(out=outd[128 * i:128 * i + 128, :], in_=ob[:, :]), deps=[t_o], dma=True)
        p4_prev = [t_w, t_o, t_f]

    stp4.close()
    stacc.close()
    with nc.allow_low_precision(reason="bf16 activation stream; reductions run in f32 internally"):
        P.emit(nc, sems)
    nc._keep = (st,)
    return nc


_CACHED = {}


def kernel(**inputs):
    if "nc" not in _CACHED:
        _CACHED["nc"] = build_program()
    nc = _CACHED["nc"]
    x = np.asarray(inputs["L0_attn_x"])
    in_maps = []
    for core in range(8):
        b, j = core // 2, core % 2
        tiles = own_tiles(j)
        idx = np.zeros((128, NOWN), np.uint32)
        pos = np.zeros((128, NOWN), np.float32)
        for i, T in enumerate(tiles):
            rows = np.arange(128 * T, 128 * T + 128)
            idx[:, i] = rows
            pos[:, i] = rows
        m = {"x": np.ascontiguousarray(x[b]), "own_idx": idx, "own_pos": pos}
        for k in ("attn_qkv_w", "attn_conv_b0",
                  "attn_conv_b1", "attn_proj_w", "attn_proj_b", "attn_ln_w",
                  "attn_ln_b", "ffn_ln1_w", "ffn_ln2_w", "ffn_wq", "ffn_wk",
                  "ffn_wv", "ffn_wo", "ffn_qnorm_w", "ffn_knorm_w",
                  "ffn_router_w", "ffn_router_bias", "ffn_wg", "ffn_wu", "ffn_wd"):
            m[k] = np.ascontiguousarray(np.asarray(inputs["L0_" + k]), dtype=np.float32)
        cv = np.zeros(256, np.float32)
        cv[0:72] = np.asarray(inputs["L0_attn_conv_w0"], np.float32).ravel()
        cv[128:200] = np.asarray(inputs["L0_attn_conv_w1"], np.float32).ravel()
        m["conv_all"] = cv
        in_maps.append(m)
    res = run_bass_kernel_spmd(nc, in_maps, core_ids=list(range(8)))
    out = np.zeros((B, S, D), np.float32)
    for core in range(8):
        b, j = core // 2, core % 2
        r = res.results[core]["out"]
        for i, T in enumerate(own_tiles(j)):
            out[b, 128 * T:128 * T + 128, :] = r[128 * i:128 * i + 128, :]
    return out
